# version 05 (final)
# baseline (speedup 1.0000x reference)
import numpy as np
import ml_dtypes
from contextlib import ExitStack

import concourse.bass as bass
import concourse.mybir as mybir
from concourse.bass_utils import run_bass_kernel_spmd

F32 = mybir.dt.float32
BF16 = mybir.dt.bfloat16
AX = mybir.AxisListType
ALU = mybir.AluOpType
ACT = mybir.ActivationFunctionType

B, L, D = 4, 8192, 1024
NF = 16384
T_CORE = 4096
NTILE = 32
NCH = 64
NCHUNK = 8
EPS = 1e-5


def _make_consts():
    n1 = np.arange(128)
    n2 = np.arange(64)
    k2_65 = np.arange(65)
    k1 = np.arange(128)
    th = 2 * np.pi * np.outer(n2, k1) / 128.0
    F1cs = np.concatenate([np.cos(th), -np.sin(th)], 1).astype(np.float32)
    EFW = np.zeros((128, 128, 195), np.float32)
    for q in range(128):
        a = 2 * np.pi * np.outer(n1, q + 128 * k2_65) / NF
        EFW[:, q, 0:65] = np.cos(a)
        EFW[:, q, 65:130] = np.sin(a)
        EFW[:, q, 130:195] = -np.sin(a)
    nn = np.concatenate([np.arange(1, 65), [0]])
    EMK = np.zeros((65, 128, 130), np.float32)
    for q in range(128):
        a = 2 * np.pi * np.outer(nn, q + 128 * k2_65) / NF
        EMK[:, q, 0:65] = np.cos(a)
        EMK[:, q, 65:130] = -np.sin(a)
    ph = 2 * np.pi * np.outer(np.arange(64), n1) / 128.0
    FA = np.zeros((128, 256), np.float32)
    FA[0:64, 0:128] = np.cos(ph)
    FA[64:128, 0:128] = -np.sin(ph)
    FA[0:64, 128:256] = np.sin(ph)
    FA[64:128, 128:256] = np.cos(ph)
    EINV = np.zeros((128, 128, 128), np.float32)
    for p in range(128):
        a = 2 * np.pi * np.outer(k1, p / float(NF) + n2 / 128.0)
        EINV[:, p, 0:64] = np.cos(a)
        EINV[:, p, 64:128] = -np.sin(a)
    ident = np.eye(128, dtype=np.float32)
    return dict(F1cs=F1cs, EFW=EFW.reshape(128, 128 * 195),
                EMK=EMK.reshape(65, 128 * 130), FA=FA,
                EINV=EINV.reshape(128, 128 * 128), ident=ident)


def build_program(dbg=()):
    nc = bass.Bass("TRN2", target_bir_lowering=False, detect_race_conditions=False)

    x_in = nc.dram_tensor("x_slab", [T_CORE, D], F32, kind="ExternalInput").ap()
    rwt_in = nc.dram_tensor("rwt", [D, 8], F32, kind="ExternalInput").ap()
    rbb_in = nc.dram_tensor("rb_bcast", [128, 8], F32, kind="ExternalInput").ap()
    ewt_in = nc.dram_tensor("ewt", [8, D, D], F32, kind="ExternalInput").ap()
    eb_in = nc.dram_tensor("eb", [8, D], F32, kind="ExternalInput").ap()
    ca_in = nc.dram_tensor("coef_a", [128, 64], F32, kind="ExternalInput").ap()
    cb_in = nc.dram_tensor("coef_b", [128, 64], F32, kind="ExternalInput").ap()
    h0_in = nc.dram_tensor("h0_sl", [128, 1], F32, kind="ExternalInput").ap()
    F1_in = nc.dram_tensor("F1cs_i", [64, 256], F32, kind="ExternalInput").ap()
    EFW_in = nc.dram_tensor("EFW_i", [128, 128 * 195], F32, kind="ExternalInput").ap()
    EMK_in = nc.dram_tensor("EMK_i", [65, 128 * 130], F32, kind="ExternalInput").ap()
    FA_in = nc.dram_tensor("FA_i", [128, 256], F32, kind="ExternalInput").ap()
    EINV_in = nc.dram_tensor("EINV_i", [128, 128 * 128], F32, kind="ExternalInput").ap()
    id_in = nc.dram_tensor("ident_i", [128, 128], F32, kind="ExternalInput").ap()
    out_t = nc.dram_tensor("out", [T_CORE, D], F32, kind="ExternalOutput").ap()

    w_bf = nc.dram_tensor("w_bf", [8, D, D], BF16)
    xh_tm = nc.dram_tensor("xh_tm", [T_CORE, D], BF16)
    xl_tm = nc.dram_tensor("xl_tm", [T_CORE, D], BF16)
    xl_fm = nc.dram_tensor("xl_fm", [D, T_CORE], BF16)
    ag1_in = nc.dram_tensor("ag1_in", [D, T_CORE], BF16)
    ag1_out = nc.dram_tensor("ag1_out", [2, D, T_CORE], BF16)
    agk_in = nc.dram_tensor("agk_in", [2, 65, 128, 128], BF16)
    agk_out = nc.dram_tensor("agk_out", [4, 2, 65, 128, 128], BF16)
    conv_hbm = nc.dram_tensor("conv_hbm", [L, 512], BF16)
    ag2_out = nc.dram_tensor("ag2_out", [2, L, 512], BF16)

    dbg_o = {}
    for name, shape in (("d_xh", [T_CORE, D]), ("d_xl", [T_CORE, D]),
                        ("d_s", [128, NTILE * 8]), ("d_kf", [2, 65, 128, 128]),
                        ("d_conv", [L, 512]), ("d_t1", [128, NCH * 256]),
                        ("d_y", [128, 128 * NCH]), ("d_c", [128, NCH * 256]),
                        ("d_xch", [64, NCH * 128]), ("d_xchf", [64, NCH * 128]), ("d_ag1", [2, 64, T_CORE]), ("d_agin", [64, T_CORE]),
                        ("d_lg", [8, T_CORE]), ("d_lgs", [128, NTILE * 8])):
        if name in dbg:
            dbg_o[name] = nc.dram_tensor(name, shape, F32, kind="ExternalOutput").ap()

    es = ExitStack()
    S = lambda name, shape, dt: es.enter_context(nc.sbuf_tensor(name, shape, dt))

    def SEMS(*names):
        return [es.enter_context(nc.semaphore(n)) for n in names]

    with es:
        ident32 = S("ident32", [128, 128], F32)
        FAc = S("FA_c", [128, 256], BF16)
        rw32 = S("rw32", [128, 64], F32)
        Rh = S("Rh", [128, 64], BF16)
        Rl = S("Rl", [128, 64], BF16)
        rbb = S("rbb", [128, 8], F32)
        bb = S("bb", [8, D], BF16)
        ones64 = S("ones64", [1, 64], BF16)

        (smP0, smCst, smW, smVx, smKfT, smKfE, smKfD, smKfSt,
         smLd1, smSu, smStat, smLd2, smXn, smHi, smLo, smLnSt, smXp,
         smCC, smCvLd, smKld, smS1, smS1e, smS2, smS2e, smMu, smIa, smIae,
         smIb, smIbe, smCvSt,
         smRtLd, smRtMm, smRtFm, smTp, smLgs, smSel, smSeT, smSTb,
         smFLd, smBias, smInit, smGrp, smEvac, smOut, smCv8, smDbg, smDb2, smXpS, smDb3) = SEMS(
            "smP0", "smCst", "smW", "smVx", "smKfT", "smKfE", "smKfD", "smKfSt",
            "smLd1", "smSu", "smStat", "smLd2", "smXn", "smHi", "smLo", "smLnSt", "smXp",
            "smCC", "smCvLd", "smKld", "smS1", "smS1e", "smS2", "smS2e", "smMu", "smIa", "smIae",
            "smIb", "smIbe", "smCvSt",
            "smRtLd", "smRtMm", "smRtFm", "smTp", "smLgs", "smSel", "smSeT", "smSTb",
            "smFLd", "smBias", "smInit", "smGrp", "smEvac", "smOut", "smCv8", "smDbg", "smDb2", "smXpS", "smDb3")

        blk = ExitStack()
        block = blk.enter_context(nc.Block())
        hreg = {}

        def get_h(e, key):
            if key not in hreg:
                hreg[key] = e.partition_id() % 2
            return hreg[key]

        conv_es = ExitStack()
        CS = lambda name, shape, dt: conv_es.enter_context(nc.sbuf_tensor(name, shape, dt))
        F1cs = CS("F1cs_c", [64, 256], BF16)
        EFW = CS("EFW_c", [128, 128 * 195], BF16)
        EINV = CS("EINV_c", [128, 128 * 128], BF16)

        kf_es = ExitStack()
        KS = lambda name, shape, dt: kf_es.enter_context(nc.sbuf_tensor(name, shape, dt))
        EMK = KS("EMK_c", [65, 128 * 130], BF16)
        eb32 = KS("eb32", [8, D], F32)
        ca = KS("ca", [128, 64], F32)
        cb = KS("cb", [128, 64], F32)
        h0 = KS("h0", [128, 1], F32)
        bprime = KS("bprime", [128, 64], F32)
        coefAB = KS("coefAB", [65, 256], BF16)
        kpart = [KS(f"kpart{i}", [65, 2 * 16 * 128], BF16) for i in range(2)]
        sbA = [KS(f"sbA{i}", [65, 512], F32) for i in range(2)]
        sbB = [KS(f"sbB{i}", [65, 512], F32) for i in range(2)]
        dv = [KS(f"dv{i}", [65, 256], F32) for i in range(6)]

        pk_es = ExitStack()
        _pkb = [pk_es.enter_context(nc.psum_tensor(f"pkb{i}", [128, 512], F32)) for i in range(6)]
        pA = [_pkb[0][0:65, 0:256], _pkb[1][0:65, 0:256]]
        pB = [_pkb[2][0:65, 0:256], _pkb[3][0:65, 0:256]]
        pT = _pkb[4][0:64, 0:128]
        pT1 = _pkb[5][0:1, 0:128]

        ln_es = ExitStack()
        LS = lambda name, shape, dt: ln_es.enter_context(nc.sbuf_tensor(name, shape, dt))
        xring = [LS(f"xr{i}", [128, D], F32) for i in range(4)]
        junk = LS("junk", [128, D], F32)
        sums0 = LS("sums0", [128, NTILE], F32)
        sums1 = LS("sums1", [128, NTILE], F32)
        m_sl = LS("m_sl", [128, NTILE], F32)
        r_sl = LS("r_sl", [128, NTILE], F32)
        tA = LS("tA", [128, NTILE], F32)
        xn32 = [LS(f"xn32_{i}", [128, D], F32) for i in range(2)]
        xh_t = [LS(f"xh_t{i}", [128, D], BF16) for i in range(2)]
        xl_t = [LS(f"xl_t{i}", [128, D], BF16) for i in range(2)]
        fmbuf = [LS(f"fmb{i}", [128, T_CORE], BF16) for i in range(2)]

        @block.sync
        def _(e):
            e.dma_start(out=ident32[:], in_=id_in).then_inc(smP0, 16)
            e.dma_start(out=rw32[:].rearrange("p (c e) -> p c e", c=8),
                        in_=rwt_in.rearrange("(c p) e -> p c e", p=128)).then_inc(smP0, 16)
            e.dma_start(out=rbb[:], in_=rbb_in).then_inc(smP0, 16)
            e.dma_start(out=eb32[:], in_=eb_in).then_inc(smP0, 16)
            e.dma_start(out=ca[:], in_=ca_in).then_inc(smP0, 16)
            e.dma_start(out=cb[:], in_=cb_in).then_inc(smP0, 16)
            e.dma_start(out=h0[:], in_=h0_in).then_inc(smP0, 16)
        P0N = 7 * 16

        @block.gpsimd
        def _(g):
            g.dma_start(out=F1cs[:], in_=F1_in).then_inc(smCst, 16)
            g.dma_start(out=FAc[:], in_=FA_in).then_inc(smCst, 16)
            g.dma_start(out=EMK[:], in_=EMK_in).then_inc(smCst, 16)
            g.dma_start(out=EFW[:], in_=EFW_in).then_inc(smCst, 16)
            g.dma_start(out=EINV[:], in_=EINV_in).then_inc(smCst, 16)
            g.dma_start(out=w_bf.ap(), in_=ewt_in).then_inc(smW, 16)
        CSTN = 5 * 16

        @block.vector
        def _(v):
            v.wait_ge(smP0, P0N)
            v.tensor_copy(Rh[:], rw32[:])
            v.tensor_tensor(Rl[:], rw32[:], Rh[:], op=ALU.subtract)
            v.tensor_copy(bb[:], eb32[:])
            v.memset(ones64[:], 1.0)
            v.tensor_scalar(bprime[:], ca[:], h0[:], None, op0=ALU.mult)
            v.tensor_tensor(bprime[:], bprime[:], cb[:], op=ALU.add)
            v.memset(coefAB[64:65, 0:128], 1.0)
            v.drain().then_inc(smVx, 1)

        @block.tensor
        def _(t):
            t.wait_ge(smVx, 1)
            t.transpose(pT, ca[:], ident32[:]).then_inc(smKfT, 1)

        @block.scalar
        def _(sc):
            sc.wait_ge(smKfT, 1)
            sc.copy(coefAB[0:64, 0:128], pT)
            sc.drain().then_inc(smKfE, 1)

        @block.tensor
        def _(t):
            t.wait_ge(smKfE, 1)
            t.transpose(pT, bprime[:], ident32[:])
            t.transpose(pT1, h0[:], ident32[:]).then_inc(smKfT, 1)

        @block.scalar
        def _(sc):
            sc.wait_ge(smKfT, 2)
            sc.copy(coefAB[0:64, 128:256], pT)
            sc.copy(coefAB[64:65, 128:256], pT1)
            sc.drain().then_inc(smKfE, 1)

        emkv = EMK[:].rearrange("p (q c) -> p q c", q=128)

        @block.tensor
        def _(t):
            t.wait_ge(smKfE, 2)
            t.wait_ge(smCst, 3 * 16)
            for q in range(128):
                pb = q % 2
                if q >= 2:
                    t.wait_ge(smKfE, 2 + (q - 2) + 1)
                t.matmul(pA[pb], emkv[:, q, 0:65], coefAB[:], start=True, stop=True)
                t.matmul(pB[pb], emkv[:, q, 65:130], coefAB[:], start=True, stop=True).then_inc(smKfT, 1)

        @block.scalar
        def _(sc):
            for q in range(128):
                p_ = q // 2
                sc.wait_ge(smKfT, 2 + q + 1)
                if p_ >= 2:
                    sc.wait_ge(smKfD, p_ - 1)
                sc.copy(sbA[p_ % 2][:, (q % 2) * 256:(q % 2) * 256 + 256], pA[q % 2])
                sc.copy(sbB[p_ % 2][:, (q % 2) * 256:(q % 2) * 256 + 256], pB[q % 2]).then_inc(smKfE, 1)

        kpv = [k[:].rearrange("p (r q c) -> p r q c", r=2, q=16) for k in kpart]

        @block.vector
        def _(v):
            for p_ in range(64):
                q0 = 2 * p_
                blkid = q0 // 16
                v.wait_ge(smKfE, 2 + q0 + 2)
                if q0 % 16 == 0 and blkid >= 2:
                    v.wait_ge(smKfSt, (blkid - 1) * 16)
                sa = sbA[p_ % 2][:].rearrange("p (q h c) -> p q h c", q=2, h=2)
                sb = sbB[p_ % 2][:].rearrange("p (q h c) -> p q h c", q=2, h=2)
                ar, br = sa[:, :, 0, :], sa[:, :, 1, :]
                ai, bi = sb[:, :, 0, :], sb[:, :, 1, :]
                d3 = [d[:].rearrange("p (q c) -> p q c", q=2) for d in dv]
                v.tensor_tensor(d3[0], ar, ar, op=ALU.mult)
                v.tensor_tensor(d3[1], ai, ai, op=ALU.mult)
                v.tensor_tensor(dv[0][:], dv[0][:], dv[1][:], op=ALU.add)
                v.reciprocal(dv[1][:], dv[0][:])
                v.tensor_tensor(d3[2], br, ar, op=ALU.mult)
                v.tensor_tensor(d3[3], bi, ai, op=ALU.mult)
                v.tensor_tensor(dv[2][:], dv[2][:], dv[3][:], op=ALU.add)
                v.tensor_tensor(d3[4], bi, ar, op=ALU.mult)
                v.tensor_tensor(d3[5], br, ai, op=ALU.mult)
                v.tensor_tensor(dv[4][:], dv[4][:], dv[5][:], op=ALU.subtract)
                kv = kpv[blkid % 2]
                v.tensor_tensor(kv[:, 0, (q0 % 16):(q0 % 16) + 2, :], d3[2], d3[1], op=ALU.mult)
                if q0 % 16 == 14:
                    v.tensor_tensor(kv[:, 1, (q0 % 16):(q0 % 16) + 2, :], d3[4], d3[1], op=ALU.mult)
                    v.drain().then_inc(smKfD, 1)
                else:
                    v.tensor_tensor(kv[:, 1, (q0 % 16):(q0 % 16) + 2, :], d3[4], d3[1], op=ALU.mult).then_inc(smKfD, 1)

        @block.gpsimd
        def _(g):
            for blkid in range(8):
                g.wait_ge(smKfD, (blkid + 1) * 8)
                g.dma_start(
                    out=agk_in.ap()[:, :, blkid * 16:(blkid + 1) * 16, :]
                        .rearrange("r a k c -> a r k c"),
                    in_=kpart[blkid % 2][:].rearrange("p (r k c) -> p r k c", r=2, k=16)
                ).then_inc(smKfSt, 16)
        KFSTN = 8 * 16

        @block.sync
        def _(e):
            for i in range(NTILE):
                if i >= 4:
                    e.wait_ge(smSu, i - 4 + 1)
                e.dma_start(out=xring[i % 4][:], in_=x_in[i * 128:(i + 1) * 128, :]).then_inc(smLd1, 16)

        @block.scalar
        def _(sc):
            for i in range(NTILE):
                sc.wait_ge(smLd1, (i + 1) * 16)
                sc.activation(junk[:], xring[i % 4][:], ACT.Copy, accum_out=sums0[:, i:i + 1])
                sc.activation(junk[:], xring[i % 4][:], ACT.Square, accum_out=sums1[:, i:i + 1]).then_inc(smSu, 1)

        @block.vector
        def _(v):
            v.wait_ge(smSu, NTILE)
            v.tensor_scalar(m_sl[:], sums0[:], 1.0 / D, None, op0=ALU.mult)
            v.tensor_scalar(tA[:], sums1[:], 1.0 / D, None, op0=ALU.mult)
            v.tensor_tensor(r_sl[:], m_sl[:], m_sl[:], op=ALU.mult)
            v.tensor_tensor(tA[:], tA[:], r_sl[:], op=ALU.subtract)
            v.tensor_scalar(tA[:], tA[:], EPS, None, op0=ALU.add).then_inc(smStat, 1)

        @block.scalar
        def _(sc):
            sc.wait_ge(smStat, 1)
            sc.activation(r_sl[:], tA[:], ACT.Ln)
            sc.activation(r_sl[:], r_sl[:], ACT.Exp, scale=-0.5).then_inc(smStat, 1)

        @block.sync
        def _(e):
            e.wait_ge(smSu, NTILE)
            for i in range(NTILE + 2):
                if i < NTILE:
                    if i >= 4:
                        e.wait_ge(smXn, i - 4 + 1)
                    e.dma_start(out=xring[i % 4][:], in_=x_in[i * 128:(i + 1) * 128, :]).then_inc(smLd2, 16)
                if i >= 2:
                    k = i - 2
                    e.wait_ge(smLo, k + 1)
                    e.dma_start(out=xh_tm.ap()[k * 128:(k + 1) * 128, :], in_=xh_t[k % 2][:]).then_inc(smLnSt, 16)
                    e.dma_start(out=xl_tm.ap()[k * 128:(k + 1) * 128, :], in_=xl_t[k % 2][:]).then_inc(smLnSt, 16)

        @block.vector
        def _(v):
            v.wait_ge(smStat, 2)
            v.drain()
            for i in range(NTILE + 1):
                if i < NTILE:
                    v.wait_ge(smLd2, (i + 1) * 16)
                    if i >= 2:
                        v.wait_ge(smHi, i - 2 + 1)
                    v.tensor_scalar(xn32[i % 2][:], xring[i % 4][:], m_sl[:, i:i + 1], r_sl[:, i:i + 1],
                                    op0=ALU.subtract, op1=ALU.mult)
                    v.drain().then_inc(smXn, 1)
                if i >= 1:
                    k = i - 1
                    v.wait_ge(smHi, k + 1)
                    if k >= 2:
                        v.wait_ge(smLnSt, (k - 2 + 1) * 32)
                    v.tensor_tensor(xl_t[k % 2][:], xn32[k % 2][:], xh_t[k % 2][:],
                                    op=ALU.subtract)
                    v.drain().then_inc(smLo, 1)

        @block.scalar
        def _(sc):
            for i in range(NTILE):
                sc.wait_ge(smXn, i + 1)
                if i >= 2:
                    sc.wait_ge(smLnSt, (i - 2 + 1) * 32)
                sc.copy(xh_t[i % 2][:], xn32[i % 2][:])
                sc.drain().then_inc(smHi, 1)

        @block.sync
        def _(e):
            e.wait_ge(smLnSt, NTILE * 32)
            for k in range(16):
                src = xh_tm if k < 8 else xl_tm
                if k >= 2:
                    e.wait_ge(smXpS, (k - 2 + 1) * 16)
                e.dma_start(out=fmbuf[k % 2][:], in_=src.ap()[:, (k % 8) * 128:(k % 8 + 1) * 128],
                            transpose=True).then_inc(smXp, 16)

        @block.gpsimd
        def _(g):
            for k in range(16):
                dst = ag1_in if k < 8 else xl_fm
                dc = k % 8
                g.wait_ge(smXp, (k + 1) * 16)
                g.dma_start(out=dst.ap()[dc * 128:(dc + 1) * 128, :], in_=fmbuf[k % 2][:]).then_inc(smXpS, 16)
        XPN = 16 * 16
        AG1_READY = 8 * 16

        @block.gpsimd
        def _(g):
            g.wait_ge(smXpS, AG1_READY)
            g.collective_compute("AllGather", ALU.bypass,
                                 replica_groups=[[0, 1], [2, 3], [4, 5], [6, 7]],
                                 ins=[ag1_in.ap()], outs=[ag1_out.ap()]).then_inc(smCC, 1)
            g.wait_ge(smKfSt, KFSTN)
            g.collective_compute("AllGather", ALU.bypass,
                                 replica_groups=[[0, 2, 4, 6], [1, 3, 5, 7]],
                                 ins=[agk_in.ap()], outs=[agk_out.ap()]).then_inc(smCC, 1)

        ln_es.close()
        kf_es.close()
        pk_es.close()

        cv_es = ExitStack()
        CV = lambda name, shape, dt: cv_es.enter_context(nc.sbuf_tensor(name, shape, dt))
        uXC = CV("uXC", [128, NCH * 256], BF16)
        T1 = CV("T1", [128, NCH * 256], BF16)
        Kq_re = [CV(f"Kqre{i}", [65, 32 * NCH], BF16) for i in range(2)]
        Kq_im = [CV(f"Kqim{i}", [65, 32 * NCH], BF16) for i in range(2)]
        xsb_re = [CV(f"xsbre{i}", [65, 256], BF16) for i in range(2)]
        xsb_im = [CV(f"xsbim{i}", [65, 256], BF16) for i in range(2)]
        tm1 = CV("tm1", [64, 256], BF16)
        tm2 = CV("tm2", [64, 256], BF16)
        Ybuf = CV("Ybuf", [128, 128 * NCH], BF16)
        conv_sb = [CV(f"conv_sb{i}", [64, 64 * NCH], BF16) for i in range(2)]
        xnyq_re = CV("xnyq_re", [1, 64], BF16)
        xnyq_im = CV("xnyq_im", [1, 64], BF16)
        knyq_re = CV("knyq_re", [1, 64], BF16)
        knyq_im = CV("knyq_im", [1, 64], BF16)
        ynyq = CV("ynyq", [1, 64], F32)
        y0row = CV("y0row", [1, 64], F32)
        corr_e = CV("corr_e", [1, 64], BF16)
        corr_o = CV("corr_o", [1, 64], BF16)
        tc1 = CV("tc1", [1, 64], F32)

        cv_ps = ExitStack()
        _cvb = [cv_ps.enter_context(nc.psum_tensor(f"cvb{i}", [128, 512], F32)) for i in range(8)]
        pS1 = [_cvb[0][:, 0:256], _cvb[1][:, 0:256]]
        pXre = [_cvb[2][0:65, 0:256], _cvb[3][0:65, 0:256]]
        pXim = [_cvb[4][0:65, 0:256], _cvb[5][0:65, 0:256]]
        pIA = [_cvb[6][:, 0:256], _cvb[7][:, 0:256]]

        EFWv = EFW[:].rearrange("p (q c) -> p q c", q=128)
        EINVv = EINV[:].rearrange("p (q c) -> p q c", q=128)
        T1v = T1[:].rearrange("p (c k) -> p c k", c=NCH)
        xchv = uXC[0:64, 0:NCH * 128].rearrange("p (c n) -> p c n", c=NCH)
        Cbv = uXC[:].rearrange("p (c k) -> p c k", c=NCH)

        for j in range(NCHUNK):
            slot, sub = j // 2, (j % 2) * 64

            def emit_xch(e, engname, sl, j):
                h = get_h(e, engname)
                if j == 0:
                    e.wait_ge(smCC, 1)
                    e.wait_ge(smKfSt, KFSTN)
                    e.wait_ge(smXpS, XPN)
                else:
                    e.wait_ge(smS1, j * 64)
                    e.wait_ge(smIb, j * 32)
                ag1w = ag1_out.ap().rearrange("s (hh j c) (a n) -> s hh j a c n",
                                              hh=2, j=NCHUNK, n=128)
                e.dma_start(
                    out=uXC[sl * 32:(sl + 1) * 32, 0:NCH * 128]
                        .rearrange("a (c n) -> a c n", n=128),
                    in_=ag1w[sl, bass.ds(h, 1), j, :, :, :]).then_inc(smCvLd, 16)

            @block.sync
            def _(e, j=j):
                emit_xch(e, "sync", 0, j)

            @block.scalar
            def _(e, j=j):
                emit_xch(e, "scalar", 1, j)

            def emit_kld(e, qt, j=j, slot=slot, sub=sub):
                Q = j * 4 + qt
                if Q >= 2:
                    e.wait_ge(smMu, (Q - 1) * 8)
                kre3 = Kq_re[Q % 2][:].rearrange("p (q c) -> p q c", q=32)
                kim3 = Kq_im[Q % 2][:].rearrange("p (q c) -> p q c", q=32)
                e.dma_start(out=kre3,
                            in_=agk_out.ap()[slot, 0, :, qt * 32:(qt + 1) * 32, sub:sub + 64]).then_inc(smKld, 16)
                e.dma_start(out=kim3,
                            in_=agk_out.ap()[slot, 1, :, qt * 32:(qt + 1) * 32, sub:sub + 64]).then_inc(smKld, 16)

            @block.scalar
            def _(e, j=j):
                if j == 0:
                    e.wait_ge(smCC, 2)
                    e.wait_ge(smXpS, XPN)
                emit_kld(e, 0)
                emit_kld(e, 1)

            @block.tensor
            def _(t, j=j):
                t.wait_ge(smCvLd, j * 32 + 32)
                if j == NCHUNK - 1 and "d_xchf" in dbg_o:
                    t.wait_ge(smDb2, 16)
                if j == 0:
                    t.wait_ge(smCst, CSTN)
                else:
                    t.wait_ge(smS2, j * 32)
                    t.wait_ge(smIbe, j * 32)
                for ch in range(NCH):
                    if ch >= 2:
                        t.wait_ge(smS1e, j * 64 + ch - 2 + 1)
                    t.matmul(pS1[ch % 2], xchv[:, ch, :], F1cs[:], start=True, stop=True).then_inc(smS1, 1)

            @block.scalar
            def _(sc, j=j):
                if j > 0:
                    sc.wait_ge(smS2, j * 32)
                for ch in range(NCH):
                    sc.wait_ge(smS1, j * 64 + ch + 1)
                    if ch == NCH - 1:
                        sc.copy(T1v[:, ch, :], pS1[ch % 2])
                        sc.drain().then_inc(smS1e, 1)
                    else:
                        sc.copy(T1v[:, ch, :], pS1[ch % 2]).then_inc(smS1e, 1)

            @block.tensor
            def _(t, j=j):
                t.wait_ge(smS1e, j * 64 + NCH)
                for g in range(32):
                    pb = g % 2
                    if g >= 2:
                        t.wait_ge(smS2e, j * 32 + g - 2 + 1)
                    elif j > 0:
                        t.wait_ge(smS2e, j * 32)
                    for qi in range(4):
                        q = g * 4 + qi
                        ncol = 65 if q == 0 else 64
                        reg_re = pXre[pb][0:ncol, qi * 64:(qi + 1) * 64]
                        reg_im = pXim[pb][0:ncol, qi * 64:(qi + 1) * 64]
                        t.matmul(reg_re, EFWv[:, q, 0:ncol], T1v[:, :, q], start=True, stop=False)
                        t.matmul(reg_re, EFWv[:, q, 65:65 + ncol], T1v[:, :, 128 + q], start=False, stop=True)
                        t.matmul(reg_im, EFWv[:, q, 0:ncol], T1v[:, :, 128 + q], start=True, stop=False)
                        ii = t.matmul(reg_im, EFWv[:, q, 130:130 + ncol], T1v[:, :, q], start=False, stop=True)
                    ii.then_inc(smS2, 1)

            @block.scalar
            def _(sc, j=j):
                for g in range(32):
                    if g == 8:
                        emit_kld(sc, 2)
                    elif g == 16:
                        emit_kld(sc, 3)
                    pb = g % 2
                    sc.wait_ge(smS2, j * 32 + g + 1)
                    if g >= 2:
                        sc.wait_ge(smMu, j * 32 + g - 2 + 1)
                    sc.copy(xsb_re[pb][0:64, :], pXre[pb][0:64, :])
                    ii = sc.copy(xsb_im[pb][0:64, :], pXim[pb][0:64, :])
                    if g == 0:
                        sc.copy(xnyq_re[:], pXre[pb][64:65, 0:64])
                        ii = sc.copy(xnyq_im[:], pXim[pb][64:65, 0:64])
                    ii.then_inc(smS2e, 1)

            @block.vector
            def _(v, j=j):
                if j > 0:
                    v.wait_ge(smIa, j * 64)
                    v.wait_ge(smIb, j * 32)
                for g in range(32):
                    pb = g % 2
                    Q = j * 4 + g // 8
                    v.wait_ge(smKld, (Q + 1) * 32)
                    v.wait_ge(smS2e, j * 32 + g + 1)
                    kcols = slice((g % 8) * 256, (g % 8 + 1) * 256)
                    kr = Kq_re[Q % 2][0:64, kcols]
                    ki = Kq_im[Q % 2][0:64, kcols]
                    cols = slice(g * 256, (g + 1) * 256)
                    v.tensor_tensor(tm1[:], xsb_re[pb][0:64, :], kr, op=ALU.mult)
                    v.tensor_tensor(tm2[:], xsb_im[pb][0:64, :], ki, op=ALU.mult)
                    v.tensor_tensor(Ybuf[0:64, cols], tm1[:], tm2[:], op=ALU.subtract)
                    v.tensor_tensor(tm1[:], xsb_re[pb][0:64, :], ki, op=ALU.mult)
                    v.tensor_tensor(tm2[:], xsb_im[pb][0:64, :], kr, op=ALU.mult)
                    ii = v.tensor_tensor(Ybuf[64:128, cols], tm1[:], tm2[:], op=ALU.add)
                    if g == 0:
                        v.tensor_copy(knyq_re[:], Kq_re[Q % 2][64:65, 0:64])
                        v.tensor_copy(knyq_im[:], Kq_im[Q % 2][64:65, 0:64])
                        v.tensor_tensor(tc1[:], xnyq_re[:], knyq_re[:], op=ALU.mult)
                        v.tensor_tensor(ynyq[:], xnyq_im[:], knyq_im[:], op=ALU.mult)
                        v.tensor_tensor(ynyq[:], tc1[:], ynyq[:], op=ALU.subtract)
                        v.tensor_copy(y0row[:], Ybuf[0:1, 0:64])
                        v.tensor_tensor(tc1[:], ynyq[:], y0row[:], op=ALU.subtract)
                        v.tensor_scalar(corr_e[:], tc1[:], 0.5, None, op0=ALU.mult)
                        v.tensor_tensor(tc1[:], ynyq[:], y0row[:], op=ALU.add)
                        ii = v.tensor_scalar(corr_o[:], tc1[:], -0.5, None, op0=ALU.mult)
                    if g == 31:
                        v.drain().then_inc(smMu, 1)
                    else:
                        ii.then_inc(smMu, 1)

            @block.tensor
            def _(t, j=j):
                t.wait_ge(smMu, j * 32 + 32)
                t.wait_ge(smS1, j * 64 + NCH)
                for ch in range(NCH):
                    if ch >= 2:
                        t.wait_ge(smIae, j * 64 + ch - 2 + 1)
                    elif j > 0:
                        t.wait_ge(smIae, j * 64)
                    lhs = Ybuf[:, ch:NCH * 128:64]
                    t.matmul(pIA[ch % 2], lhs, FAc[:], start=True, stop=True).then_inc(smIa, 1)

            @block.scalar
            def _(sc, j=j):
                for ch in range(NCH):
                    sc.wait_ge(smIa, j * 64 + ch + 1)
                    if ch == NCH - 1:
                        sc.copy(Cbv[:, ch, :], pIA[ch % 2])
                        sc.drain().then_inc(smIae, 1)
                    else:
                        sc.copy(Cbv[:, ch, :], pIA[ch % 2]).then_inc(smIae, 1)

            @block.tensor
            def _(t, j=j):
                t.wait_ge(smIae, j * 64 + NCH)
                for g in range(32):
                    pb = g % 2
                    if g >= 2:
                        t.wait_ge(smIbe, j * 32 + g - 2 + 1)
                    for pi in range(4):
                        p = g * 4 + pi
                        reg = pS1[pb][0:64, pi * 64:(pi + 1) * 64]
                        t.matmul(reg, EINVv[:, p, 0:64], Cbv[:, :, p], start=True, stop=False)
                        t.matmul(reg, EINVv[:, p, 64:128], Cbv[:, :, 128 + p], start=False, stop=False)
                        crow = corr_e if p % 2 == 0 else corr_o
                        ii = t.matmul(reg, ones64[:], crow[:], start=False, stop=True)
                    ii.then_inc(smIb, 1)

            @block.vector
            def _(v, j=j):
                for g in range(32):
                    pb = g % 2
                    half = g // 16
                    M = j * 2 + half
                    if g % 16 == 0 and M >= 2:
                        v.wait_ge(smCvSt, (M - 1) * 16)
                    v.wait_ge(smIb, j * 32 + g + 1)
                    cols = slice((g % 16) * 256, (g % 16 + 1) * 256)
                    if g % 16 == 15:
                        v.tensor_scalar(conv_sb[M % 2][:, cols], pS1[pb][0:64, :], 2.0 / NF, None,
                                        op0=ALU.mult)
                        v.drain().then_inc(smIbe, 1)
                    else:
                        v.tensor_scalar(conv_sb[M % 2][:, cols], pS1[pb][0:64, :], 2.0 / NF, None,
                                        op0=ALU.mult).then_inc(smIbe, 1)

            if j == NCHUNK - 1 and "d_xchf" in dbg_o:
                @block.gpsimd
                def _(g):
                    g.wait_ge(smCvLd, NCHUNK * 32)
                    g.dma_start(out=dbg_o["d_xchf"], in_=uXC[0:64, 0:NCH * 128]).then_inc(smDb2, 16)

            @block.gpsimd
            def _(g, j=j):
                cview = conv_hbm.ap()[:, j * 64:(j + 1) * 64].rearrange("(a n) c -> a n c", n=128)
                for half in range(2):
                    M = j * 2 + half
                    g.wait_ge(smIbe, j * 32 + (half + 1) * 16)
                    g.dma_start(out=cview[:, half * 64:(half + 1) * 64, :],
                                in_=conv_sb[M % 2][:].rearrange("p (n c) -> p n c", n=64)).then_inc(smCvSt, 16)

        CONV_T_IB = NCHUNK * 32
        CONV_V_IBE = NCHUNK * 32
        CVSTN = NCHUNK * 32

        n_db2 = 0
        if "d_t1" in dbg_o or "d_y" in dbg_o or "d_c" in dbg_o or "d_xch" in dbg_o or "d_xchf" in dbg_o:
            @block.gpsimd
            def _(g):
                if "d_xch" in dbg_o:
                    g.wait_ge(smS1, NCHUNK * 64)
                    g.dma_start(out=dbg_o["d_xch"], in_=uXC[0:64, 0:NCH * 128]).then_inc(smDb2, 16)

                if "d_t1" in dbg_o:
                    g.wait_ge(smS2, NCHUNK * 32)
                    g.dma_start(out=dbg_o["d_t1"], in_=T1[:]).then_inc(smDb2, 16)
                if "d_y" in dbg_o:
                    g.wait_ge(smIa, NCHUNK * 64)
                    g.dma_start(out=dbg_o["d_y"], in_=Ybuf[:]).then_inc(smDb2, 16)
                if "d_c" in dbg_o:
                    g.wait_ge(smIb, NCHUNK * 32)
                    g.dma_start(out=dbg_o["d_c"], in_=uXC[:]).then_inc(smDb2, 16)
            n_db2 = 16 * sum(1 for k in ("d_xch", "d_xchf", "d_t1", "d_y", "d_c") if k in dbg_o)

        @block.gpsimd
        def _(g):
            g.wait_ge(smCvSt, CVSTN)
            g.wait_ge(smDb2, n_db2)
            g.collective_compute("AllGather", ALU.bypass,
                                 replica_groups=[[0, 1], [2, 3], [4, 5], [6, 7]],
                                 ins=[conv_hbm.ap()], outs=[ag2_out.ap()]).then_inc(smCC, 1)

        cv_ps.close()
        cv_es.close()
        conv_es.close()

        rf_es = ExitStack()
        RF = lambda name, shape, dt: rf_es.enter_context(nc.sbuf_tensor(name, shape, dt))
        lgs = RF("lgs", [128, NTILE * 8], F32)
        negm = RF("negm", [128, NTILE], F32)
        expv = RF("expv", [128, NTILE * 8], F32)
        den = RF("den", [128, NTILE], F32)
        wrec = RF("wrec", [128, NTILE], F32)
        inds = RF("inds", [128, NTILE * 8], F32)
        s_sl = RF("s_sl", [128, NTILE * 8], F32)
        sT = RF("sT", [8, NTILE * 128], BF16)

        rt_es = ExitStack()
        RT = lambda name, shape, dt: rt_es.enter_context(nc.sbuf_tensor(name, shape, dt))
        lgfm = RT("lgfm", [8, T_CORE], F32)
        xhr = [RT(f"xhr{i}", [128, 8 * 512], BF16) for i in range(2)]
        xlr = [RT(f"xlr{i}", [128, 8 * 512], BF16) for i in range(2)]

        rt_ps = ExitStack()
        _rtb = [rt_ps.enter_context(nc.psum_tensor(f"rtb{i}", [128, 512], F32)) for i in range(5)]
        pR = _rtb[0][0:8, :]
        pTr = [_rtb[1][:, 0:8], _rtb[2][:, 0:8]]
        pSt = [_rtb[3][0:8, 0:128], _rtb[4][0:8, 0:128]]

        @block.sync
        def _(e):
            e.wait_ge(smXpS, XPN)
            e.wait_ge(smDb2, n_db2)
            e.wait_ge(smCvSt, CVSTN)
            e.wait_ge(smIbe, CONV_V_IBE)
            e.wait_ge(smIb, CONV_T_IB)
            e.wait_ge(smMu, NCHUNK * 32)
            for ts in range(8):
                if ts >= 2:
                    e.wait_ge(smRtMm, ts - 2 + 1)
                for cch in range(8):
                    e.dma_start(
                        out=xhr[ts % 2][:, cch * 512:(cch + 1) * 512],
                        in_=ag1_in.ap()[cch * 128:(cch + 1) * 128,
                                        ts * 512:(ts + 1) * 512]).then_inc(smRtLd, 16)
                for cch in range(8):
                    e.dma_start(
                        out=xlr[ts % 2][:, cch * 512:(cch + 1) * 512],
                        in_=xl_fm.ap()[cch * 128:(cch + 1) * 128,
                                       ts * 512:(ts + 1) * 512]).then_inc(smRtLd, 16)

        @block.tensor
        def _(t):
            t.wait_ge(smIb, CONV_T_IB)
            t.wait_ge(smIbe, CONV_V_IBE)
            t.wait_ge(smIae, NCHUNK * 64)
            t.wait_ge(smMu, NCHUNK * 32)
            for ts in range(8):
                t.wait_ge(smRtLd, min(8, ts + 2) * 256)
                if ts >= 1:
                    t.wait_ge(smRtFm, ts)
                xhv = xhr[ts % 2][:].rearrange("p (c t) -> p c t", c=8)
                xlv = xlr[ts % 2][:].rearrange("p (c t) -> p c t", c=8)
                first = True
                for dc in range(8):
                    for ci, (Rm, xm) in enumerate(((Rh, xhv), (Rh, xlv), (Rl, xhv))):
                        ii = t.matmul(pR, Rm[:, dc * 8:(dc + 1) * 8], xm[:, dc, :],
                                      start=first, stop=(dc == 7 and ci == 2))
                        first = False
                ii.then_inc(smRtMm, 1)

        @block.vector
        def _(v):
            for ts in range(8):
                v.wait_ge(smRtMm, ts + 1)
                v.tensor_copy(lgfm[:, ts * 512:(ts + 1) * 512], pR)
                v.drain().then_inc(smRtFm, 1)

        @block.tensor
        def _(t):
            t.wait_ge(smRtFm, 8)
            for i in range(NTILE):
                if i >= 2:
                    t.wait_ge(smLgs, i - 2 + 1)
                t.transpose(pTr[i % 2], lgfm[:, i * 128:(i + 1) * 128], ident32[0:8, 0:8]).then_inc(smTp, 1)

        @block.vector
        def _(v):
            for i in range(NTILE):
                v.wait_ge(smTp, i + 1)
                v.tensor_tensor(lgs[:, i * 8:(i + 1) * 8], pTr[i % 2], rbb[:], op=ALU.add).then_inc(smLgs, 1)

        lgs3 = lgs[:].rearrange("p (i e) -> p i e", e=8)
        expv3 = expv[:].rearrange("p (i e) -> p i e", e=8)

        @block.vector
        def _(v):
            v.wait_ge(smLgs, NTILE)
            v.reduce_max(negm[:], lgs3, axis=AX.X, negate=True)
            v.drain().then_inc(smSel, 1)

        @block.scalar
        def _(sc):
            sc.wait_ge(smSel, 1)
            for i in range(NTILE):
                sc.activation(expv[:, i * 8:(i + 1) * 8], lgs[:, i * 8:(i + 1) * 8],
                              ACT.Exp, bias=negm[:, i:i + 1]).then_inc(smSel, 1)
            sc.activation(expv[:, 0:8], lgs[:, 0:8], ACT.Exp, bias=negm[:, 0:1])
            sc.drain().then_inc(smSel, 1)

        @block.vector
        def _(v):
            for i in range(NTILE):
                v.tensor_scalar(inds[:, i * 8:(i + 1) * 8], lgs[:, i * 8:(i + 1) * 8],
                                negm[:, i:i + 1], 0.0, op0=ALU.add, op1=ALU.is_ge)
            v.wait_ge(smSel, 2 + NTILE)
            v.reduce_sum(den[:], expv3, axis=AX.X)
            v.drain()
            v.reciprocal(wrec[:], den[:])
            v.drain()
            for i in range(NTILE):
                if i == NTILE - 1:
                    v.tensor_scalar(s_sl[:, i * 8:(i + 1) * 8], inds[:, i * 8:(i + 1) * 8],
                                    wrec[:, i:i + 1], None, op0=ALU.mult)
                    v.drain().then_inc(smSeT, 1)
                else:
                    v.tensor_scalar(s_sl[:, i * 8:(i + 1) * 8], inds[:, i * 8:(i + 1) * 8],
                                    wrec[:, i:i + 1], None, op0=ALU.mult).then_inc(smSeT, 1)

        @block.tensor
        def _(t):
            t.wait_ge(smSeT, NTILE)
            for i in range(NTILE):
                if i >= 2:
                    t.wait_ge(smSTb, i - 2 + 1)
                t.transpose(pSt[i % 2], s_sl[:, i * 8:(i + 1) * 8], ident32[:]).then_inc(smTp, 1)

        @block.vector
        def _(v):
            for i in range(NTILE):
                v.wait_ge(smTp, NTILE + i + 1)
                if i == NTILE - 1:
                    v.tensor_copy(sT[:, i * 128:(i + 1) * 128], pSt[i % 2])
                    v.drain().then_inc(smSTb, 1)
                else:
                    v.tensor_copy(sT[:, i * 128:(i + 1) * 128], pSt[i % 2]).then_inc(smSTb, 1)

        if "d_lg" in dbg_o:
            @block.gpsimd
            def _(g):
                g.wait_ge(smRtFm, 8)
                g.dma_start(out=dbg_o["d_lg"], in_=lgfm[:]).then_inc(smDb3, 16)

        rt_ps.close()
        rt_es.close()

        ff_es = ExitStack()
        FF = lambda name, shape, dt: ff_es.enter_context(nc.sbuf_tensor(name, shape, dt))
        W_sb = FF("W_sb", [128, 8 * 8 * D], BF16)
        xh_f = [FF(f"xh_f{i}", [128, 8 * 128], BF16) for i in range(3)]
        x_f = [FF(f"x_f{i}", [128, D], F32) for i in range(2)]
        cv8 = [FF(f"cv8_{i}", [128, 8 * D], BF16) for i in range(2)]
        cv8v = [c[:].rearrange("p (t c) -> p t c", t=8) for c in cv8]
        accA = [FF(f"accA{i}", [128, D], F32) for i in range(2)]
        accB = [FF(f"accB{i}", [128, D], F32) for i in range(2)]

        ff_ps = ExitStack()
        pBias = [ff_ps.enter_context(nc.psum_tensor(f"pBias{i}", [128, 512], F32)) for i in range(2)]
        pE = [ff_ps.enter_context(nc.psum_tensor(f"pE{i}", [128, 512], F32)) for i in range(6)]

        Wv = W_sb[:].rearrange("p (e c f) -> p e c f", e=8, c=8)

        @block.sync
        def _(e):
            e.wait_ge(smW, 16)
            if "d_lg" in dbg_o:
                e.wait_ge(smDb3, 16)
            e.wait_ge(smSTb, NTILE)
            e.wait_ge(smIb, CONV_T_IB)
            e.wait_ge(smIbe, CONV_V_IBE)
            e.wait_ge(smCvSt, CVSTN)
            for ei in range(8):
                for cch in range(8):
                    e.dma_start(out=Wv[:, ei, cch, :],
                                in_=w_bf.ap()[ei, cch * 128:(cch + 1) * 128, :]).then_inc(smFLd, 16)
        WLDN = 8 * 8 * 16

        GROUPS = [((0, 1, 2), (0, 1, 2)), ((3, 4, 5), (3, 4, 5)), ((6, 7), (0, 1))]

        def evac_tok(i, fh, e):
            return i * 16 + fh * 8 + e + 1

        def fld_tok(i):
            return WLDN + (i + 1) * 144

        @block.sync
        def _(e):
            if "d_lg" in dbg_o:
                e.wait_ge(smDb3, 16)
            e.wait_ge(smSTb, NTILE)
            for i in range(NTILE + 2):
                if i < NTILE:
                    if i >= 2:
                        e.wait_ge(smInit, i - 2 + 1)
                    e.dma_start(out=x_f[i % 2][:], in_=x_in[i * 128:(i + 1) * 128, :]).then_inc(smFLd, 16)
                    if i >= 3:
                        e.wait_ge(smGrp, (i - 3 + 1) * 6)
                    for cch in range(8):
                        e.dma_start(
                            out=xh_f[i % 3][:, cch * 128:(cch + 1) * 128],
                            in_=ag1_in.ap()[cch * 128:(cch + 1) * 128,
                                            i * 128:(i + 1) * 128]).then_inc(smFLd, 16)
                pass

        @block.gpsimd
        def _(g):
            h = get_h(g, "gpsimd")
            g.wait_ge(smCC, 3)
            g.wait_ge(smSTb, NTILE)
            ag2w = ag2_out.ap().rearrange("s (hh bt t p) c -> s hh bt p t c",
                                          hh=2, bt=4, p=128)

            def cv8_batch(bt):
                if bt >= 2:
                    g.wait_ge(smInit, (bt - 1) * 8)
                for sl in range(2):
                    g.dma_start(
                        out=cv8v[bt % 2][:, :, sl * 512:(sl + 1) * 512],
                        in_=ag2w[sl, bass.ds(h, 1), bt, :, :, :]).then_inc(smCv8, 16)

            cv8_batch(0)
            cv8_batch(1)
            for i in range(NTILE):
                if i % 8 == 6 and i // 8 + 2 < 4:
                    cv8_batch(i // 8 + 2)
                g.wait_ge(smEvac, evac_tok(i, 1, 7))
                g.dma_start(out=out_t[i * 128:(i + 1) * 128, :], in_=accB[i % 2][:]).then_inc(smOut, 16)

        @block.tensor
        def _(t):
            t.wait_ge(smSTb, NTILE)
            for i in range(NTILE):
                if i >= 1:
                    t.wait_ge(smInit, i)
                t.wait_ge(smFLd, fld_tok(i))
                t.matmul(pBias[0][:], sT[:, i * 128:(i + 1) * 128], bb[:, 0:512],
                         start=True, stop=True)
                t.matmul(pBias[1][:], sT[:, i * 128:(i + 1) * 128], bb[:, 512:1024],
                         start=True, stop=True).then_inc(smBias, 1)
                xhv = xh_f[i % 3][:].rearrange("p (c t) -> p c t", c=8)
                for fh in range(2):
                    for gi, (elist, slots) in enumerate(GROUPS):
                        for e_, s_ in zip(elist, slots):
                            if gi == 2:
                                t.wait_ge(smEvac, evac_tok(i, fh, s_))
                            else:
                                pi, pfh = (i, 0) if fh == 1 else (i - 1, 1)
                                if pi >= 0:
                                    pe = {0: 6, 1: 7, 2: 2, 3: 3, 4: 4, 5: 5}[s_]
                                    t.wait_ge(smEvac, evac_tok(pi, pfh, pe))
                        for dc in range(8):
                            for e_, s_ in zip(elist, slots):
                                ii = t.matmul(pE[s_][:], xhv[:, dc, :],
                                              Wv[:, e_, dc, fh * 512:(fh + 1) * 512],
                                              start=(dc == 0), stop=(dc == 7))
                        ii.then_inc(smGrp, 1)

        @block.vector
        def _(v):
            for i in range(NTILE):
                v.wait_ge(smFLd, fld_tok(i))
                v.wait_ge(smCv8, min(4, i // 8 + 2) * 32)
                v.wait_ge(smBias, i + 1)
                if i >= 2:
                    v.wait_ge(smOut, (i - 2 + 1) * 16)
                for fh in range(2):
                    fcols = slice(fh * 512, (fh + 1) * 512)
                    v.tensor_tensor(accA[i % 2][:, fcols], x_f[i % 2][:, fcols],
                                    cv8v[(i // 8) % 2][:, i % 8, fcols], op=ALU.add)
                    ii = v.tensor_tensor(accB[i % 2][:, fcols], accA[i % 2][:, fcols],
                                         pBias[fh][:], op=ALU.add)
                ii.then_inc(smInit, 1)
                for fh in range(2):
                    fcols = slice(fh * 512, (fh + 1) * 512)
                    src, dst = accB[i % 2], accA[i % 2]
                    for gi, (elist, slots) in enumerate(GROUPS):
                        v.wait_ge(smGrp, i * 6 + fh * 3 + gi + 1)
                        for e_, s_ in zip(elist, slots):
                            if fh == 1 and e_ == 7:
                                v.scalar_tensor_tensor(dst[:, fcols], pE[s_][:],
                                                       s_sl[:, i * 8 + e_:i * 8 + e_ + 1],
                                                       src[:, fcols], op0=ALU.mult,
                                                       op1=ALU.add)
                                v.drain().then_inc(smEvac, 1)
                            else:
                                v.scalar_tensor_tensor(dst[:, fcols], pE[s_][:],
                                                       s_sl[:, i * 8 + e_:i * 8 + e_ + 1],
                                                       src[:, fcols], op0=ALU.mult,
                                                       op1=ALU.add).then_inc(smEvac, 1)
                            src, dst = dst, src

        if dbg_o:
            @block.gpsimd
            def _(g):
                if "d_xh" in dbg_o:
                    g.wait_ge(smLnSt, NTILE * 32)
                    g.dma_start(out=dbg_o["d_xh"], in_=xh_tm.ap()).then_inc(smDbg, 16)
                if "d_xl" in dbg_o:
                    g.wait_ge(smLnSt, NTILE * 32)
                    g.dma_start(out=dbg_o["d_xl"], in_=xl_tm.ap()).then_inc(smDbg, 16)
                if "d_s" in dbg_o:
                    g.wait_ge(smSeT, NTILE)
                    g.dma_start(out=dbg_o["d_s"], in_=s_sl[:]).then_inc(smDbg, 16)
                if "d_lgs" in dbg_o:
                    g.wait_ge(smLgs, NTILE)
                    g.dma_start(out=dbg_o["d_lgs"], in_=lgs[:]).then_inc(smDbg, 16)
                if "d_kf" in dbg_o:
                    g.wait_ge(smKfSt, KFSTN)
                    g.dma_start(out=dbg_o["d_kf"], in_=agk_in.ap()).then_inc(smDbg, 16)
                if "d_conv" in dbg_o:
                    g.wait_ge(smCvSt, CVSTN)
                    g.dma_start(out=dbg_o["d_conv"], in_=conv_hbm.ap()).then_inc(smDbg, 16)
                if "d_ag1" in dbg_o:
                    g.wait_ge(smCC, 1)
                    g.dma_start(out=dbg_o["d_ag1"], in_=ag1_out.ap()[:, 448:512, :]).then_inc(smDbg, 16)
                if "d_agin" in dbg_o:
                    g.wait_ge(smCC, 1)
                    g.dma_start(out=dbg_o["d_agin"], in_=ag1_in.ap()[448:512, :]).then_inc(smDbg, 16)

        @block.sync
        def _(e):
            e.wait_ge(smOut, NTILE * 16)
            if dbg_o:
                e.wait_ge(smDbg, 16 * sum(1 for k in ("d_xh", "d_xl", "d_s", "d_kf", "d_conv", "d_ag1", "d_agin", "d_lgs") if k in dbg_o))
                e.wait_ge(smDb2, n_db2)
                if "d_lg" in dbg_o:
                    e.wait_ge(smDb3, 16)

        ff_ps.close()
        ff_es.close()
        rf_es.close()
        blk.close()
    return nc


_consts_cache = {}


def _get_consts():
    if "c" not in _consts_cache:
        _consts_cache["c"] = _make_consts()
    return _consts_cache["c"]


def make_inmaps(u, ab, h_0, rw, rb, ew, eb):
    cst = _get_consts()
    u = np.ascontiguousarray(np.asarray(u, dtype=np.float32))
    ab = np.asarray(ab, dtype=np.float32)
    h_0 = np.asarray(h_0, dtype=np.float32)
    rw = np.asarray(rw, dtype=np.float32)
    rb = np.asarray(rb, dtype=np.float32)
    ew = np.asarray(ew, dtype=np.float32)
    eb = np.ascontiguousarray(np.asarray(eb, dtype=np.float32))
    rwt = np.ascontiguousarray(rw.T)
    rbb = np.ascontiguousarray(np.broadcast_to(rb[None, :], (128, 8)))
    ewt = np.ascontiguousarray(ew.transpose(0, 2, 1))
    a_co = ab[:D]
    b_co = ab[D:]
    in_maps = []
    for c in range(8):
        b_, h_ = c >> 1, c & 1
        ch0 = h_ * 512 + (c >> 1) * 128
        in_maps.append({
            "x_slab": np.ascontiguousarray(u[b_, h_ * T_CORE:(h_ + 1) * T_CORE, :]),
            "rwt": rwt, "rb_bcast": rbb, "ewt": ewt, "eb": eb,
            "coef_a": np.ascontiguousarray(a_co[ch0:ch0 + 128, :]),
            "coef_b": np.ascontiguousarray(b_co[ch0:ch0 + 128, :]),
            "h0_sl": np.ascontiguousarray(h_0[ch0:ch0 + 128, None]),
            "F1cs_i": cst["F1cs"], "EFW_i": cst["EFW"], "EMK_i": cst["EMK"],
            "FA_i": cst["FA"], "EINV_i": cst["EINV"], "ident_i": cst["ident"],
        })
    return in_maps


def kernel(L0_attn_u, L0_attn_ab, L0_attn_h_0, L0_ffn_router_w, L0_ffn_router_b,
           L0_ffn_expert_w, L0_ffn_expert_b):
    in_maps = make_inmaps(L0_attn_u, L0_attn_ab, L0_attn_h_0, L0_ffn_router_w,
                          L0_ffn_router_b, L0_ffn_expert_w, L0_ffn_expert_b)
    nc = build_program()
    res = run_bass_kernel_spmd(nc, in_maps, core_ids=list(range(8)))
    out = np.empty((B, L, D), np.float32)
    for c in range(8):
        b_, h_ = c >> 1, c & 1
        out[b_, h_ * T_CORE:(h_ + 1) * T_CORE, :] = res.results[c]["out"]
    return out


# version 06
# speedup vs baseline: 1.0140x; 1.0140x over previous
import numpy as np
import ml_dtypes
from contextlib import ExitStack

import concourse.bass as bass
import concourse.mybir as mybir
from concourse.bass_utils import run_bass_kernel_spmd

F32 = mybir.dt.float32
BF16 = mybir.dt.bfloat16
AX = mybir.AxisListType
ALU = mybir.AluOpType
ACT = mybir.ActivationFunctionType

B, L, D = 4, 8192, 1024
NF = 16384
T_CORE = 4096
NTILE = 32
NCH = 64
NCHUNK = 8
EPS = 1e-5


def _make_consts():
    n1 = np.arange(128)
    n2 = np.arange(64)
    k2_65 = np.arange(65)
    k1 = np.arange(128)
    th = 2 * np.pi * np.outer(n2, k1) / 128.0
    F1cs = np.concatenate([np.cos(th), -np.sin(th)], 1).astype(np.float32)
    EFW = np.zeros((128, 128, 195), np.float32)
    for q in range(128):
        a = 2 * np.pi * np.outer(n1, q + 128 * k2_65) / NF
        EFW[:, q, 0:65] = np.cos(a)
        EFW[:, q, 65:130] = np.sin(a)
        EFW[:, q, 130:195] = -np.sin(a)
    nn = np.concatenate([np.arange(1, 65), [0]])
    EMK = np.zeros((65, 128, 130), np.float32)
    for q in range(128):
        a = 2 * np.pi * np.outer(nn, q + 128 * k2_65) / NF
        EMK[:, q, 0:65] = np.cos(a)
        EMK[:, q, 65:130] = -np.sin(a)
    ph = 2 * np.pi * np.outer(np.arange(64), n1) / 128.0
    FA = np.zeros((128, 256), np.float32)
    FA[0:64, 0:128] = np.cos(ph)
    FA[64:128, 0:128] = -np.sin(ph)
    FA[0:64, 128:256] = np.sin(ph)
    FA[64:128, 128:256] = np.cos(ph)
    EINV = np.zeros((128, 128, 128), np.float32)
    for p in range(128):
        a = 2 * np.pi * np.outer(k1, p / float(NF) + n2 / 128.0)
        EINV[:, p, 0:64] = np.cos(a)
        EINV[:, p, 64:128] = -np.sin(a)
    ident = np.eye(128, dtype=np.float32)
    return dict(F1cs=F1cs, EFW=EFW.reshape(128, 128 * 195),
                EMK=EMK.reshape(65, 128 * 130), FA=FA,
                EINV=EINV.reshape(128, 128 * 128), ident=ident)


def build_program(dbg=()):
    nc = bass.Bass("TRN2", target_bir_lowering=False, detect_race_conditions=False)

    x_in = nc.dram_tensor("x_slab", [T_CORE, D], F32, kind="ExternalInput").ap()
    rwt_in = nc.dram_tensor("rwt", [D, 8], F32, kind="ExternalInput").ap()
    rbb_in = nc.dram_tensor("rb_bcast", [128, 8], F32, kind="ExternalInput").ap()
    ewt_in = nc.dram_tensor("ewt", [8, D, D], F32, kind="ExternalInput").ap()
    eb_in = nc.dram_tensor("eb", [8, D], F32, kind="ExternalInput").ap()
    ca_in = nc.dram_tensor("coef_a", [128, 64], F32, kind="ExternalInput").ap()
    cb_in = nc.dram_tensor("coef_b", [128, 64], F32, kind="ExternalInput").ap()
    h0_in = nc.dram_tensor("h0_sl", [128, 1], F32, kind="ExternalInput").ap()
    F1_in = nc.dram_tensor("F1cs_i", [64, 256], F32, kind="ExternalInput").ap()
    EFW_in = nc.dram_tensor("EFW_i", [128, 128 * 195], F32, kind="ExternalInput").ap()
    EMK_in = nc.dram_tensor("EMK_i", [65, 128 * 130], F32, kind="ExternalInput").ap()
    FA_in = nc.dram_tensor("FA_i", [128, 256], F32, kind="ExternalInput").ap()
    EINV_in = nc.dram_tensor("EINV_i", [128, 128 * 128], F32, kind="ExternalInput").ap()
    id_in = nc.dram_tensor("ident_i", [128, 128], F32, kind="ExternalInput").ap()
    out_t = nc.dram_tensor("out", [T_CORE, D], F32, kind="ExternalOutput").ap()

    w_bf = nc.dram_tensor("w_bf", [8, D, D], BF16)
    xh_tm = nc.dram_tensor("xh_tm", [T_CORE, D], BF16)
    xl_tm = nc.dram_tensor("xl_tm", [T_CORE, D], BF16)
    xl_fm = nc.dram_tensor("xl_fm", [D, T_CORE], BF16)
    ag1_in = nc.dram_tensor("ag1_in", [D, T_CORE], BF16)
    ag1_out = nc.dram_tensor("ag1_out", [2, D, T_CORE], BF16)
    agk_in = nc.dram_tensor("agk_in", [2, 65, 128, 128], BF16)
    agk_out = nc.dram_tensor("agk_out", [4, 2, 65, 128, 128], BF16)
    conv_hbm = nc.dram_tensor("conv_hbm", [L, 512], BF16)
    ag2_out = nc.dram_tensor("ag2_out", [2, L, 512], BF16)

    dbg_o = {}
    for name, shape in (("d_xh", [T_CORE, D]), ("d_xl", [T_CORE, D]),
                        ("d_s", [128, NTILE * 8]), ("d_kf", [2, 65, 128, 128]),
                        ("d_conv", [L, 512]), ("d_t1", [128, NCH * 256]),
                        ("d_y", [128, 128 * NCH]), ("d_c", [128, NCH * 256]),
                        ("d_xch", [64, NCH * 128]), ("d_xchf", [64, NCH * 128]), ("d_ag1", [2, 64, T_CORE]), ("d_agin", [64, T_CORE]),
                        ("d_lg", [8, T_CORE]), ("d_lgs", [128, NTILE * 8])):
        if name in dbg:
            dbg_o[name] = nc.dram_tensor(name, shape, F32, kind="ExternalOutput").ap()

    es = ExitStack()
    S = lambda name, shape, dt: es.enter_context(nc.sbuf_tensor(name, shape, dt))

    def SEMS(*names):
        return [es.enter_context(nc.semaphore(n)) for n in names]

    with es:
        ident32 = S("ident32", [128, 128], F32)
        FAc = S("FA_c", [128, 256], BF16)
        rw32 = S("rw32", [128, 64], F32)
        Rh = S("Rh", [128, 64], BF16)
        Rl = S("Rl", [128, 64], BF16)
        rbb = S("rbb", [128, 8], F32)
        bb = S("bb", [8, D], BF16)
        ones64 = S("ones64", [1, 64], BF16)

        (smP0, smCst, smW, smVx, smKfT, smKfE, smKfD, smKfSt,
         smLd1, smSu, smStat, smLd2, smXn, smHi, smLo, smLnSt, smXp,
         smCC, smCvLd, smKld, smS1, smS1e, smS2, smS2e, smMu, smIa, smIae,
         smIb, smIbe, smCvSt,
         smRtLd, smRtMm, smRtFm, smTp, smLgs, smSel, smSeT, smSTb,
         smFLd, smBias, smInit, smGrp, smEvac, smOut, smCv8, smDbg, smDb2, smXpS, smDb3) = SEMS(
            "smP0", "smCst", "smW", "smVx", "smKfT", "smKfE", "smKfD", "smKfSt",
            "smLd1", "smSu", "smStat", "smLd2", "smXn", "smHi", "smLo", "smLnSt", "smXp",
            "smCC", "smCvLd", "smKld", "smS1", "smS1e", "smS2", "smS2e", "smMu", "smIa", "smIae",
            "smIb", "smIbe", "smCvSt",
            "smRtLd", "smRtMm", "smRtFm", "smTp", "smLgs", "smSel", "smSeT", "smSTb",
            "smFLd", "smBias", "smInit", "smGrp", "smEvac", "smOut", "smCv8", "smDbg", "smDb2", "smXpS", "smDb3")

        blk = ExitStack()
        block = blk.enter_context(nc.Block())
        hreg = {}

        def get_h(e, key):
            if key not in hreg:
                hreg[key] = e.partition_id() % 2
            return hreg[key]

        conv_es = ExitStack()
        CS = lambda name, shape, dt: conv_es.enter_context(nc.sbuf_tensor(name, shape, dt))
        F1cs = CS("F1cs_c", [64, 256], BF16)
        EFW = CS("EFW_c", [128, 128 * 195], BF16)
        EINV = CS("EINV_c", [128, 128 * 128], BF16)

        kf_es = ExitStack()
        KS = lambda name, shape, dt: kf_es.enter_context(nc.sbuf_tensor(name, shape, dt))
        EMK = KS("EMK_c", [65, 128 * 130], BF16)
        eb32 = KS("eb32", [8, D], F32)
        ca = KS("ca", [128, 64], F32)
        cb = KS("cb", [128, 64], F32)
        h0 = KS("h0", [128, 1], F32)
        bprime = KS("bprime", [128, 64], F32)
        coefAB = KS("coefAB", [65, 256], BF16)
        kpart = [KS(f"kpart{i}", [65, 2 * 16 * 128], BF16) for i in range(2)]
        sbA = [KS(f"sbA{i}", [65, 512], F32) for i in range(2)]
        sbB = [KS(f"sbB{i}", [65, 512], F32) for i in range(2)]
        dv = [KS(f"dv{i}", [65, 256], F32) for i in range(6)]

        pk_es = ExitStack()
        _pkb = [pk_es.enter_context(nc.psum_tensor(f"pkb{i}", [128, 512], F32)) for i in range(6)]
        pA = [_pkb[0][0:65, 0:256], _pkb[1][0:65, 0:256]]
        pB = [_pkb[2][0:65, 0:256], _pkb[3][0:65, 0:256]]
        pT = _pkb[4][0:64, 0:128]
        pT1 = _pkb[5][0:1, 0:128]

        ln_es = ExitStack()
        LS = lambda name, shape, dt: ln_es.enter_context(nc.sbuf_tensor(name, shape, dt))
        xring = [LS(f"xr{i}", [128, D], F32) for i in range(4)]
        junk = LS("junk", [128, D], F32)
        sums0 = LS("sums0", [128, NTILE], F32)
        sums1 = LS("sums1", [128, NTILE], F32)
        m_sl = LS("m_sl", [128, NTILE], F32)
        r_sl = LS("r_sl", [128, NTILE], F32)
        tA = LS("tA", [128, NTILE], F32)
        xn32 = [LS(f"xn32_{i}", [128, D], F32) for i in range(2)]
        xh_t = [LS(f"xh_t{i}", [128, D], BF16) for i in range(2)]
        xl_t = [LS(f"xl_t{i}", [128, D], BF16) for i in range(2)]
        fmbuf = [LS(f"fmb{i}", [128, T_CORE], BF16) for i in range(2)]

        @block.sync
        def _(e):
            e.dma_start(out=ident32[:], in_=id_in).then_inc(smP0, 16)
            e.dma_start(out=rw32[:].rearrange("p (c e) -> p c e", c=8),
                        in_=rwt_in.rearrange("(c p) e -> p c e", p=128)).then_inc(smP0, 16)
            e.dma_start(out=rbb[:], in_=rbb_in).then_inc(smP0, 16)
            e.dma_start(out=eb32[:], in_=eb_in).then_inc(smP0, 16)
            e.dma_start(out=ca[:], in_=ca_in).then_inc(smP0, 16)
            e.dma_start(out=cb[:], in_=cb_in).then_inc(smP0, 16)
            e.dma_start(out=h0[:], in_=h0_in).then_inc(smP0, 16)
        P0N = 7 * 16

        @block.gpsimd
        def _(g):
            g.dma_start(out=F1cs[:], in_=F1_in).then_inc(smCst, 16)
            g.dma_start(out=FAc[:], in_=FA_in).then_inc(smCst, 16)
            g.dma_start(out=EMK[:], in_=EMK_in).then_inc(smCst, 16)
            g.dma_start(out=EFW[:], in_=EFW_in).then_inc(smCst, 16)
            g.dma_start(out=EINV[:], in_=EINV_in).then_inc(smCst, 16)
            g.dma_start(out=w_bf.ap(), in_=ewt_in).then_inc(smW, 16)
        CSTN = 5 * 16

        @block.vector
        def _(v):
            v.wait_ge(smP0, P0N)
            v.tensor_copy(Rh[:], rw32[:])
            v.tensor_tensor(Rl[:], rw32[:], Rh[:], op=ALU.subtract)
            v.tensor_copy(bb[:], eb32[:])
            v.memset(ones64[:], 1.0)
            v.tensor_scalar(bprime[:], ca[:], h0[:], None, op0=ALU.mult)
            v.tensor_tensor(bprime[:], bprime[:], cb[:], op=ALU.add)
            v.memset(coefAB[64:65, 0:128], 1.0)
            v.drain().then_inc(smVx, 1)

        @block.tensor
        def _(t):
            t.wait_ge(smVx, 1)
            t.transpose(pT, ca[:], ident32[:]).then_inc(smKfT, 1)

        @block.scalar
        def _(sc):
            sc.wait_ge(smKfT, 1)
            sc.copy(coefAB[0:64, 0:128], pT)
            sc.drain().then_inc(smKfE, 1)

        @block.tensor
        def _(t):
            t.wait_ge(smKfE, 1)
            t.transpose(pT, bprime[:], ident32[:])
            t.transpose(pT1, h0[:], ident32[:]).then_inc(smKfT, 1)

        @block.scalar
        def _(sc):
            sc.wait_ge(smKfT, 2)
            sc.copy(coefAB[0:64, 128:256], pT)
            sc.copy(coefAB[64:65, 128:256], pT1)
            sc.drain().then_inc(smKfE, 1)

        emkv = EMK[:].rearrange("p (q c) -> p q c", q=128)

        @block.tensor
        def _(t):
            t.wait_ge(smKfE, 2)
            t.wait_ge(smCst, 3 * 16)
            for q in range(128):
                pb = q % 2
                if q >= 2:
                    t.wait_ge(smKfE, 2 + (q - 2) + 1)
                t.matmul(pA[pb], emkv[:, q, 0:65], coefAB[:], start=True, stop=True)
                t.matmul(pB[pb], emkv[:, q, 65:130], coefAB[:], start=True, stop=True).then_inc(smKfT, 1)

        @block.scalar
        def _(sc):
            for q in range(128):
                p_ = q // 2
                sc.wait_ge(smKfT, 2 + q + 1)
                if p_ >= 2:
                    sc.wait_ge(smKfD, p_ - 1)
                sc.copy(sbA[p_ % 2][:, (q % 2) * 256:(q % 2) * 256 + 256], pA[q % 2])
                sc.copy(sbB[p_ % 2][:, (q % 2) * 256:(q % 2) * 256 + 256], pB[q % 2]).then_inc(smKfE, 1)

        kpv = [k[:].rearrange("p (r q c) -> p r q c", r=2, q=16) for k in kpart]

        @block.vector
        def _(v):
            for p_ in range(64):
                q0 = 2 * p_
                blkid = q0 // 16
                v.wait_ge(smKfE, 2 + q0 + 2)
                if q0 % 16 == 0 and blkid >= 2:
                    v.wait_ge(smKfSt, (blkid - 1) * 16)
                sa = sbA[p_ % 2][:].rearrange("p (q h c) -> p q h c", q=2, h=2)
                sb = sbB[p_ % 2][:].rearrange("p (q h c) -> p q h c", q=2, h=2)
                ar, br = sa[:, :, 0, :], sa[:, :, 1, :]
                ai, bi = sb[:, :, 0, :], sb[:, :, 1, :]
                d3 = [d[:].rearrange("p (q c) -> p q c", q=2) for d in dv]
                v.tensor_tensor(d3[0], ar, ar, op=ALU.mult)
                v.tensor_tensor(d3[1], ai, ai, op=ALU.mult)
                v.tensor_tensor(dv[0][:], dv[0][:], dv[1][:], op=ALU.add)
                v.reciprocal(dv[1][:], dv[0][:])
                v.tensor_tensor(d3[2], br, ar, op=ALU.mult)
                v.tensor_tensor(d3[3], bi, ai, op=ALU.mult)
                v.tensor_tensor(dv[2][:], dv[2][:], dv[3][:], op=ALU.add)
                v.tensor_tensor(d3[4], bi, ar, op=ALU.mult)
                v.tensor_tensor(d3[5], br, ai, op=ALU.mult)
                v.tensor_tensor(dv[4][:], dv[4][:], dv[5][:], op=ALU.subtract)
                kv = kpv[blkid % 2]
                v.tensor_tensor(kv[:, 0, (q0 % 16):(q0 % 16) + 2, :], d3[2], d3[1], op=ALU.mult)
                if q0 % 16 == 14:
                    v.tensor_tensor(kv[:, 1, (q0 % 16):(q0 % 16) + 2, :], d3[4], d3[1], op=ALU.mult)
                    v.drain().then_inc(smKfD, 1)
                else:
                    v.tensor_tensor(kv[:, 1, (q0 % 16):(q0 % 16) + 2, :], d3[4], d3[1], op=ALU.mult).then_inc(smKfD, 1)

        @block.gpsimd
        def _(g):
            for blkid in range(8):
                g.wait_ge(smKfD, (blkid + 1) * 8)
                g.dma_start(
                    out=agk_in.ap()[:, :, blkid * 16:(blkid + 1) * 16, :]
                        .rearrange("r a k c -> a r k c"),
                    in_=kpart[blkid % 2][:].rearrange("p (r k c) -> p r k c", r=2, k=16)
                ).then_inc(smKfSt, 16)
        KFSTN = 8 * 16

        @block.sync
        def _(e):
            for i in range(NTILE):
                if i >= 4:
                    e.wait_ge(smSu, i - 4 + 1)
                e.dma_start(out=xring[i % 4][:], in_=x_in[i * 128:(i + 1) * 128, :]).then_inc(smLd1, 16)

        @block.scalar
        def _(sc):
            for i in range(NTILE):
                sc.wait_ge(smLd1, (i + 1) * 16)
                sc.activation(junk[:], xring[i % 4][:], ACT.Copy, accum_out=sums0[:, i:i + 1])
                sc.activation(junk[:], xring[i % 4][:], ACT.Square, accum_out=sums1[:, i:i + 1]).then_inc(smSu, 1)

        @block.vector
        def _(v):
            v.wait_ge(smSu, NTILE)
            v.tensor_scalar(m_sl[:], sums0[:], 1.0 / D, None, op0=ALU.mult)
            v.tensor_scalar(tA[:], sums1[:], 1.0 / D, None, op0=ALU.mult)
            v.tensor_tensor(r_sl[:], m_sl[:], m_sl[:], op=ALU.mult)
            v.tensor_tensor(tA[:], tA[:], r_sl[:], op=ALU.subtract)
            v.tensor_scalar(tA[:], tA[:], EPS, None, op0=ALU.add).then_inc(smStat, 1)

        @block.scalar
        def _(sc):
            sc.wait_ge(smStat, 1)
            sc.activation(r_sl[:], tA[:], ACT.Ln)
            sc.activation(r_sl[:], r_sl[:], ACT.Exp, scale=-0.5).then_inc(smStat, 1)

        @block.sync
        def _(e):
            e.wait_ge(smSu, NTILE)
            for i in range(NTILE + 2):
                if i < NTILE:
                    if i >= 4:
                        e.wait_ge(smXn, i - 4 + 1)
                    e.dma_start(out=xring[i % 4][:], in_=x_in[i * 128:(i + 1) * 128, :]).then_inc(smLd2, 16)
                if i >= 2:
                    k = i - 2
                    e.wait_ge(smLo, k + 1)
                    e.dma_start(out=xh_tm.ap()[k * 128:(k + 1) * 128, :], in_=xh_t[k % 2][:]).then_inc(smLnSt, 16)
                    e.dma_start(out=xl_tm.ap()[k * 128:(k + 1) * 128, :], in_=xl_t[k % 2][:]).then_inc(smLnSt, 16)

        @block.vector
        def _(v):
            v.wait_ge(smStat, 2)
            v.drain()
            for i in range(NTILE + 1):
                if i < NTILE:
                    v.wait_ge(smLd2, (i + 1) * 16)
                    if i >= 2:
                        v.wait_ge(smHi, i - 2 + 1)
                    v.tensor_scalar(xn32[i % 2][:], xring[i % 4][:], m_sl[:, i:i + 1], r_sl[:, i:i + 1],
                                    op0=ALU.subtract, op1=ALU.mult)
                    v.drain().then_inc(smXn, 1)
                if i >= 1:
                    k = i - 1
                    v.wait_ge(smHi, k + 1)
                    if k >= 2:
                        v.wait_ge(smLnSt, (k - 2 + 1) * 32)
                    v.tensor_tensor(xl_t[k % 2][:], xn32[k % 2][:], xh_t[k % 2][:],
                                    op=ALU.subtract)
                    v.drain().then_inc(smLo, 1)

        @block.scalar
        def _(sc):
            for i in range(NTILE):
                sc.wait_ge(smXn, i + 1)
                if i >= 2:
                    sc.wait_ge(smLnSt, (i - 2 + 1) * 32)
                sc.copy(xh_t[i % 2][:], xn32[i % 2][:])
                sc.drain().then_inc(smHi, 1)

        @block.sync
        def _(e):
            e.wait_ge(smLnSt, NTILE * 32)
            for k in range(16):
                src = xh_tm if k < 8 else xl_tm
                if k >= 2:
                    e.wait_ge(smXpS, (k - 2 + 1) * 16)
                e.dma_start(out=fmbuf[k % 2][:], in_=src.ap()[:, (k % 8) * 128:(k % 8 + 1) * 128],
                            transpose=True).then_inc(smXp, 16)

        @block.gpsimd
        def _(g):
            for k in range(16):
                dst = ag1_in if k < 8 else xl_fm
                dc = k % 8
                g.wait_ge(smXp, (k + 1) * 16)
                g.dma_start(out=dst.ap()[dc * 128:(dc + 1) * 128, :], in_=fmbuf[k % 2][:]).then_inc(smXpS, 16)
        XPN = 16 * 16
        AG1_READY = 8 * 16

        @block.gpsimd
        def _(g):
            g.wait_ge(smXpS, AG1_READY)
            g.collective_compute("AllGather", ALU.bypass,
                                 replica_groups=[[0, 1], [2, 3], [4, 5], [6, 7]],
                                 ins=[ag1_in.ap()], outs=[ag1_out.ap()]).then_inc(smCC, 1)
            g.wait_ge(smKfSt, KFSTN)
            g.collective_compute("AllGather", ALU.bypass,
                                 replica_groups=[[0, 2, 4, 6], [1, 3, 5, 7]],
                                 ins=[agk_in.ap()], outs=[agk_out.ap()]).then_inc(smCC, 1)

        ln_es.close()
        kf_es.close()
        pk_es.close()

        cv_es = ExitStack()
        CV = lambda name, shape, dt: cv_es.enter_context(nc.sbuf_tensor(name, shape, dt))
        uXC = CV("uXC", [128, NCH * 256], BF16)
        T1 = CV("T1", [128, NCH * 256], BF16)
        Kq_re = [CV(f"Kqre{i}", [65, 32 * NCH], BF16) for i in range(2)]
        Kq_im = [CV(f"Kqim{i}", [65, 32 * NCH], BF16) for i in range(2)]
        xsb_re = [CV(f"xsbre{i}", [65, 256], BF16) for i in range(2)]
        xsb_im = [CV(f"xsbim{i}", [65, 256], BF16) for i in range(2)]
        tm1 = CV("tm1", [64, 256], BF16)
        tm2 = CV("tm2", [64, 256], BF16)
        Ybuf = CV("Ybuf", [128, 128 * NCH], BF16)
        conv_sb = [CV(f"conv_sb{i}", [64, 64 * NCH], BF16) for i in range(2)]
        xnyq_re = CV("xnyq_re", [1, 64], BF16)
        xnyq_im = CV("xnyq_im", [1, 64], BF16)
        knyq_re = CV("knyq_re", [1, 64], BF16)
        knyq_im = CV("knyq_im", [1, 64], BF16)
        ynyq = CV("ynyq", [1, 64], F32)
        y0row = CV("y0row", [1, 64], F32)
        corr_e = CV("corr_e", [1, 64], BF16)
        corr_o = CV("corr_o", [1, 64], BF16)
        tc1 = CV("tc1", [1, 64], F32)

        cv_ps = ExitStack()
        _cvb = [cv_ps.enter_context(nc.psum_tensor(f"cvb{i}", [128, 512], F32)) for i in range(8)]
        pS1 = [_cvb[0][:, 0:256], _cvb[1][:, 0:256]]
        pXre = [_cvb[2][0:65, 0:256], _cvb[3][0:65, 0:256]]
        pXim = [_cvb[4][0:65, 0:256], _cvb[5][0:65, 0:256]]
        pIA = [_cvb[6][:, 0:256], _cvb[7][:, 0:256]]

        EFWv = EFW[:].rearrange("p (q c) -> p q c", q=128)
        EINVv = EINV[:].rearrange("p (q c) -> p q c", q=128)
        T1v = T1[:].rearrange("p (c k) -> p c k", c=NCH)
        xchv = uXC[0:64, 0:NCH * 128].rearrange("p (c n) -> p c n", c=NCH)
        Cbv = uXC[:].rearrange("p (c k) -> p c k", c=NCH)

        for j in range(NCHUNK):
            slot, sub = j // 2, (j % 2) * 64

            def emit_xch(e, engname, sl, j):
                h = get_h(e, engname)
                if j == 0:
                    e.wait_ge(smCC, 1)
                    e.wait_ge(smKfSt, KFSTN)
                    e.wait_ge(smXpS, XPN)
                else:
                    e.wait_ge(smS1, j * 64)
                    e.wait_ge(smIb, j * 32)
                ag1w = ag1_out.ap().rearrange("s (hh j c) (a n) -> s hh j a c n",
                                              hh=2, j=NCHUNK, n=128)
                e.dma_start(
                    out=uXC[sl * 32:(sl + 1) * 32, 0:NCH * 128]
                        .rearrange("a (c n) -> a c n", n=128),
                    in_=ag1w[sl, bass.ds(h, 1), j, :, :, :]).then_inc(smCvLd, 16)

            @block.sync
            def _(e, j=j):
                emit_xch(e, "sync", 0, j)

            @block.scalar
            def _(e, j=j):
                emit_xch(e, "scalar", 1, j)

            def emit_kld(e, qt, j=j, slot=slot, sub=sub):
                Q = j * 4 + qt
                if Q >= 2:
                    e.wait_ge(smMu, (Q - 1) * 8)
                kre3 = Kq_re[Q % 2][:].rearrange("p (q c) -> p q c", q=32)
                kim3 = Kq_im[Q % 2][:].rearrange("p (q c) -> p q c", q=32)
                e.dma_start(out=kre3,
                            in_=agk_out.ap()[slot, 0, :, qt * 32:(qt + 1) * 32, sub:sub + 64]).then_inc(smKld, 16)
                e.dma_start(out=kim3,
                            in_=agk_out.ap()[slot, 1, :, qt * 32:(qt + 1) * 32, sub:sub + 64]).then_inc(smKld, 16)

            @block.scalar
            def _(e, j=j):
                if j == 0:
                    e.wait_ge(smCC, 2)
                    e.wait_ge(smXpS, XPN)
                emit_kld(e, 0)
                emit_kld(e, 1)

            @block.tensor
            def _(t, j=j):
                t.wait_ge(smCvLd, j * 32 + 32)
                if j == NCHUNK - 1 and "d_xchf" in dbg_o:
                    t.wait_ge(smDb2, 16)
                if j == 0:
                    t.wait_ge(smCst, CSTN)
                else:
                    t.wait_ge(smS2, j * 32)
                    t.wait_ge(smIbe, j * 32)
                for ch in range(NCH):
                    if ch >= 2:
                        t.wait_ge(smS1e, j * 64 + ch - 2 + 1)
                    t.matmul(pS1[ch % 2], xchv[:, ch, :], F1cs[:], start=True, stop=True).then_inc(smS1, 1)

            @block.scalar
            def _(sc, j=j):
                if j > 0:
                    sc.wait_ge(smS2, j * 32)
                for ch in range(NCH):
                    sc.wait_ge(smS1, j * 64 + ch + 1)
                    if ch == NCH - 1:
                        sc.copy(T1v[:, ch, :], pS1[ch % 2])
                        sc.drain().then_inc(smS1e, 1)
                    else:
                        sc.copy(T1v[:, ch, :], pS1[ch % 2]).then_inc(smS1e, 1)

            @block.tensor
            def _(t, j=j):
                t.wait_ge(smS1e, j * 64 + NCH)
                for g in range(32):
                    pb = g % 2
                    if g >= 2:
                        t.wait_ge(smS2e, j * 32 + g - 2 + 1)
                    elif j > 0:
                        t.wait_ge(smS2e, j * 32)
                    for qi in range(4):
                        q = g * 4 + qi
                        ncol = 65 if q == 0 else 64
                        reg_re = pXre[pb][0:ncol, qi * 64:(qi + 1) * 64]
                        reg_im = pXim[pb][0:ncol, qi * 64:(qi + 1) * 64]
                        t.matmul(reg_re, EFWv[:, q, 0:ncol], T1v[:, :, q], start=True, stop=False)
                        t.matmul(reg_re, EFWv[:, q, 65:65 + ncol], T1v[:, :, 128 + q], start=False, stop=True)
                        t.matmul(reg_im, EFWv[:, q, 0:ncol], T1v[:, :, 128 + q], start=True, stop=False)
                        ii = t.matmul(reg_im, EFWv[:, q, 130:130 + ncol], T1v[:, :, q], start=False, stop=True)
                    ii.then_inc(smS2, 1)

            @block.scalar
            def _(sc, j=j):
                for g in range(32):
                    if g == 8:
                        emit_kld(sc, 2)
                    elif g == 16:
                        emit_kld(sc, 3)
                    pb = g % 2
                    sc.wait_ge(smS2, j * 32 + g + 1)
                    if g >= 2:
                        sc.wait_ge(smMu, j * 32 + g - 2 + 1)
                    sc.copy(xsb_re[pb][0:64, :], pXre[pb][0:64, :])
                    ii = sc.copy(xsb_im[pb][0:64, :], pXim[pb][0:64, :])
                    if g == 0:
                        sc.copy(xnyq_re[:], pXre[pb][64:65, 0:64])
                        ii = sc.copy(xnyq_im[:], pXim[pb][64:65, 0:64])
                    ii.then_inc(smS2e, 1)

            @block.vector
            def _(v, j=j):
                if j > 0:
                    v.wait_ge(smIa, j * 64)
                    v.wait_ge(smIb, j * 32)
                for g in range(32):
                    pb = g % 2
                    Q = j * 4 + g // 8
                    v.wait_ge(smKld, (Q + 1) * 32)
                    v.wait_ge(smS2e, j * 32 + g + 1)
                    kcols = slice((g % 8) * 256, (g % 8 + 1) * 256)
                    kr = Kq_re[Q % 2][0:64, kcols]
                    ki = Kq_im[Q % 2][0:64, kcols]
                    cols = slice(g * 256, (g + 1) * 256)
                    v.tensor_tensor(tm1[:], xsb_re[pb][0:64, :], kr, op=ALU.mult)
                    v.tensor_tensor(tm2[:], xsb_im[pb][0:64, :], ki, op=ALU.mult)
                    v.tensor_tensor(Ybuf[0:64, cols], tm1[:], tm2[:], op=ALU.subtract)
                    v.tensor_tensor(tm1[:], xsb_re[pb][0:64, :], ki, op=ALU.mult)
                    v.tensor_tensor(tm2[:], xsb_im[pb][0:64, :], kr, op=ALU.mult)
                    ii = v.tensor_tensor(Ybuf[64:128, cols], tm1[:], tm2[:], op=ALU.add)
                    if g == 0:
                        v.tensor_copy(knyq_re[:], Kq_re[Q % 2][64:65, 0:64])
                        v.tensor_copy(knyq_im[:], Kq_im[Q % 2][64:65, 0:64])
                        v.tensor_tensor(tc1[:], xnyq_re[:], knyq_re[:], op=ALU.mult)
                        v.tensor_tensor(ynyq[:], xnyq_im[:], knyq_im[:], op=ALU.mult)
                        v.tensor_tensor(ynyq[:], tc1[:], ynyq[:], op=ALU.subtract)
                        v.tensor_copy(y0row[:], Ybuf[0:1, 0:64])
                        v.tensor_tensor(tc1[:], ynyq[:], y0row[:], op=ALU.subtract)
                        v.tensor_scalar(corr_e[:], tc1[:], 0.5, None, op0=ALU.mult)
                        v.tensor_tensor(tc1[:], ynyq[:], y0row[:], op=ALU.add)
                        ii = v.tensor_scalar(corr_o[:], tc1[:], -0.5, None, op0=ALU.mult)
                    if g == 31:
                        v.drain().then_inc(smMu, 1)
                    else:
                        ii.then_inc(smMu, 1)

            @block.tensor
            def _(t, j=j):
                t.wait_ge(smMu, j * 32 + 32)
                t.wait_ge(smS1, j * 64 + NCH)
                for ch in range(NCH):
                    if ch >= 2:
                        t.wait_ge(smIae, j * 64 + ch - 2 + 1)
                    elif j > 0:
                        t.wait_ge(smIae, j * 64)
                    lhs = Ybuf[:, ch:NCH * 128:64]
                    t.matmul(pIA[ch % 2], lhs, FAc[:], start=True, stop=True).then_inc(smIa, 1)

            @block.scalar
            def _(sc, j=j):
                for ch in range(NCH):
                    sc.wait_ge(smIa, j * 64 + ch + 1)
                    if ch == NCH - 1:
                        sc.copy(Cbv[:, ch, :], pIA[ch % 2])
                        sc.drain().then_inc(smIae, 1)
                    else:
                        sc.copy(Cbv[:, ch, :], pIA[ch % 2]).then_inc(smIae, 1)

            @block.tensor
            def _(t, j=j):
                t.wait_ge(smIae, j * 64 + NCH)
                for g in range(32):
                    pb = g % 2
                    if g >= 2:
                        t.wait_ge(smIbe, j * 32 + g - 2 + 1)
                    for pi in range(4):
                        p = g * 4 + pi
                        reg = pS1[pb][0:64, pi * 64:(pi + 1) * 64]
                        t.matmul(reg, EINVv[:, p, 0:64], Cbv[:, :, p], start=True, stop=False)
                        t.matmul(reg, EINVv[:, p, 64:128], Cbv[:, :, 128 + p], start=False, stop=False)
                        crow = corr_e if p % 2 == 0 else corr_o
                        ii = t.matmul(reg, ones64[:], crow[:], start=False, stop=True)
                    ii.then_inc(smIb, 1)

            @block.vector
            def _(v, j=j):
                for g in range(32):
                    pb = g % 2
                    half = g // 16
                    M = j * 2 + half
                    if g % 16 == 0 and M >= 2:
                        v.wait_ge(smCvSt, (M - 1) * 16)
                    v.wait_ge(smIb, j * 32 + g + 1)
                    cols = slice((g % 16) * 256, (g % 16 + 1) * 256)
                    if g % 16 == 15:
                        v.tensor_scalar(conv_sb[M % 2][:, cols], pS1[pb][0:64, :], 2.0 / NF, None,
                                        op0=ALU.mult)
                        v.drain().then_inc(smIbe, 1)
                    else:
                        v.tensor_scalar(conv_sb[M % 2][:, cols], pS1[pb][0:64, :], 2.0 / NF, None,
                                        op0=ALU.mult).then_inc(smIbe, 1)

            if j == NCHUNK - 1 and "d_xchf" in dbg_o:
                @block.gpsimd
                def _(g):
                    g.wait_ge(smCvLd, NCHUNK * 32)
                    g.dma_start(out=dbg_o["d_xchf"], in_=uXC[0:64, 0:NCH * 128]).then_inc(smDb2, 16)

            @block.gpsimd
            def _(g, j=j):
                cview = conv_hbm.ap()[:, j * 64:(j + 1) * 64].rearrange("(a n) c -> a n c", n=128)
                for half in range(2):
                    M = j * 2 + half
                    g.wait_ge(smIbe, j * 32 + (half + 1) * 16)
                    g.dma_start(out=cview[:, half * 64:(half + 1) * 64, :],
                                in_=conv_sb[M % 2][:].rearrange("p (n c) -> p n c", n=64)).then_inc(smCvSt, 16)

        CONV_T_IB = NCHUNK * 32
        CONV_V_IBE = NCHUNK * 32
        CVSTN = NCHUNK * 32

        n_db2 = 0
        if "d_t1" in dbg_o or "d_y" in dbg_o or "d_c" in dbg_o or "d_xch" in dbg_o or "d_xchf" in dbg_o:
            @block.gpsimd
            def _(g):
                if "d_xch" in dbg_o:
                    g.wait_ge(smS1, NCHUNK * 64)
                    g.dma_start(out=dbg_o["d_xch"], in_=uXC[0:64, 0:NCH * 128]).then_inc(smDb2, 16)

                if "d_t1" in dbg_o:
                    g.wait_ge(smS2, NCHUNK * 32)
                    g.dma_start(out=dbg_o["d_t1"], in_=T1[:]).then_inc(smDb2, 16)
                if "d_y" in dbg_o:
                    g.wait_ge(smIa, NCHUNK * 64)
                    g.dma_start(out=dbg_o["d_y"], in_=Ybuf[:]).then_inc(smDb2, 16)
                if "d_c" in dbg_o:
                    g.wait_ge(smIb, NCHUNK * 32)
                    g.dma_start(out=dbg_o["d_c"], in_=uXC[:]).then_inc(smDb2, 16)
            n_db2 = 16 * sum(1 for k in ("d_xch", "d_xchf", "d_t1", "d_y", "d_c") if k in dbg_o)

        @block.gpsimd
        def _(g):
            g.wait_ge(smCvSt, CVSTN)
            g.wait_ge(smDb2, n_db2)
            g.collective_compute("AllGather", ALU.bypass,
                                 replica_groups=[[0, 1], [2, 3], [4, 5], [6, 7]],
                                 ins=[conv_hbm.ap()], outs=[ag2_out.ap()]).then_inc(smCC, 1)

        cv_ps.close()
        cv_es.close()
        conv_es.close()

        rf_es = ExitStack()
        RF = lambda name, shape, dt: rf_es.enter_context(nc.sbuf_tensor(name, shape, dt))
        lgs = RF("lgs", [128, NTILE * 8], F32)
        negm = RF("negm", [128, NTILE], F32)
        expv = RF("expv", [128, NTILE * 8], F32)
        den = RF("den", [128, NTILE], F32)
        wrec = RF("wrec", [128, NTILE], F32)
        inds = RF("inds", [128, NTILE * 8], F32)
        s_sl = RF("s_sl", [128, NTILE * 8], F32)
        sT = RF("sT", [8, NTILE * 128], BF16)

        rt_es = ExitStack()
        RT = lambda name, shape, dt: rt_es.enter_context(nc.sbuf_tensor(name, shape, dt))
        lgfm = RT("lgfm", [8, T_CORE], F32)
        xhr = [RT(f"xhr{i}", [128, 8 * 512], BF16) for i in range(2)]
        xlr = [RT(f"xlr{i}", [128, 8 * 512], BF16) for i in range(2)]

        rt_ps = ExitStack()
        _rtb = [rt_ps.enter_context(nc.psum_tensor(f"rtb{i}", [128, 512], F32)) for i in range(5)]
        pR = _rtb[0][0:8, :]
        pTr = [_rtb[1][:, 0:8], _rtb[2][:, 0:8]]
        pSt = [_rtb[3][0:8, 0:128], _rtb[4][0:8, 0:128]]

        @block.sync
        def _(e):
            e.wait_ge(smXpS, XPN)
            e.wait_ge(smDb2, n_db2)
            e.wait_ge(smCvSt, CVSTN)
            e.wait_ge(smIbe, CONV_V_IBE)
            e.wait_ge(smIb, CONV_T_IB)
            e.wait_ge(smMu, NCHUNK * 32)
            for ts in range(8):
                if ts >= 2:
                    e.wait_ge(smRtMm, ts - 2 + 1)
                for cch in range(8):
                    e.dma_start(
                        out=xhr[ts % 2][:, cch * 512:(cch + 1) * 512],
                        in_=ag1_in.ap()[cch * 128:(cch + 1) * 128,
                                        ts * 512:(ts + 1) * 512]).then_inc(smRtLd, 16)

        @block.scalar
        def _(e):
            e.wait_ge(smXpS, XPN)
            e.wait_ge(smCvSt, CVSTN)
            e.wait_ge(smIbe, CONV_V_IBE)
            e.wait_ge(smIb, CONV_T_IB)
            e.wait_ge(smMu, NCHUNK * 32)
            for ts in range(8):
                if ts >= 2:
                    e.wait_ge(smRtMm, ts - 2 + 1)
                for cch in range(8):
                    e.dma_start(
                        out=xlr[ts % 2][:, cch * 512:(cch + 1) * 512],
                        in_=xl_fm.ap()[cch * 128:(cch + 1) * 128,
                                       ts * 512:(ts + 1) * 512]).then_inc(smRtLd, 16)

        @block.tensor
        def _(t):
            t.wait_ge(smIb, CONV_T_IB)
            t.wait_ge(smIbe, CONV_V_IBE)
            t.wait_ge(smIae, NCHUNK * 64)
            t.wait_ge(smMu, NCHUNK * 32)
            for ts in range(8):
                t.wait_ge(smRtLd, min(8, ts + 2) * 256)
                if ts >= 1:
                    t.wait_ge(smRtFm, ts)
                xhv = xhr[ts % 2][:].rearrange("p (c t) -> p c t", c=8)
                xlv = xlr[ts % 2][:].rearrange("p (c t) -> p c t", c=8)
                first = True
                for dc in range(8):
                    for ci, (Rm, xm) in enumerate(((Rh, xhv), (Rh, xlv), (Rl, xhv))):
                        ii = t.matmul(pR, Rm[:, dc * 8:(dc + 1) * 8], xm[:, dc, :],
                                      start=first, stop=(dc == 7 and ci == 2))
                        first = False
                ii.then_inc(smRtMm, 1)

        @block.vector
        def _(v):
            for ts in range(8):
                v.wait_ge(smRtMm, ts + 1)
                v.tensor_copy(lgfm[:, ts * 512:(ts + 1) * 512], pR)
                v.drain().then_inc(smRtFm, 1)

        @block.tensor
        def _(t):
            t.wait_ge(smRtFm, 8)
            for i in range(NTILE):
                if i >= 2:
                    t.wait_ge(smLgs, i - 2 + 1)
                t.transpose(pTr[i % 2], lgfm[:, i * 128:(i + 1) * 128], ident32[0:8, 0:8]).then_inc(smTp, 1)

        @block.vector
        def _(v):
            for i in range(NTILE):
                v.wait_ge(smTp, i + 1)
                v.tensor_tensor(lgs[:, i * 8:(i + 1) * 8], pTr[i % 2], rbb[:], op=ALU.add).then_inc(smLgs, 1)

        lgs3 = lgs[:].rearrange("p (i e) -> p i e", e=8)
        expv3 = expv[:].rearrange("p (i e) -> p i e", e=8)

        @block.vector
        def _(v):
            v.wait_ge(smLgs, NTILE)
            v.reduce_max(negm[:], lgs3, axis=AX.X, negate=True)
            v.drain().then_inc(smSel, 1)

        @block.scalar
        def _(sc):
            sc.wait_ge(smSel, 1)
            for i in range(NTILE):
                sc.activation(expv[:, i * 8:(i + 1) * 8], lgs[:, i * 8:(i + 1) * 8],
                              ACT.Exp, bias=negm[:, i:i + 1]).then_inc(smSel, 1)
            sc.activation(expv[:, 0:8], lgs[:, 0:8], ACT.Exp, bias=negm[:, 0:1])
            sc.drain().then_inc(smSel, 1)

        @block.vector
        def _(v):
            for i in range(NTILE):
                v.tensor_scalar(inds[:, i * 8:(i + 1) * 8], lgs[:, i * 8:(i + 1) * 8],
                                negm[:, i:i + 1], 0.0, op0=ALU.add, op1=ALU.is_ge)
            v.wait_ge(smSel, 2 + NTILE)
            v.reduce_sum(den[:], expv3, axis=AX.X)
            v.drain()
            v.reciprocal(wrec[:], den[:])
            v.drain()
            for i in range(NTILE):
                if i == NTILE - 1:
                    v.tensor_scalar(s_sl[:, i * 8:(i + 1) * 8], inds[:, i * 8:(i + 1) * 8],
                                    wrec[:, i:i + 1], None, op0=ALU.mult)
                    v.drain().then_inc(smSeT, 1)
                else:
                    v.tensor_scalar(s_sl[:, i * 8:(i + 1) * 8], inds[:, i * 8:(i + 1) * 8],
                                    wrec[:, i:i + 1], None, op0=ALU.mult).then_inc(smSeT, 1)

        @block.tensor
        def _(t):
            t.wait_ge(smSeT, NTILE)
            for i in range(NTILE):
                if i >= 2:
                    t.wait_ge(smSTb, i - 2 + 1)
                t.transpose(pSt[i % 2], s_sl[:, i * 8:(i + 1) * 8], ident32[:]).then_inc(smTp, 1)

        @block.vector
        def _(v):
            for i in range(NTILE):
                v.wait_ge(smTp, NTILE + i + 1)
                if i == NTILE - 1:
                    v.tensor_copy(sT[:, i * 128:(i + 1) * 128], pSt[i % 2])
                    v.drain().then_inc(smSTb, 1)
                else:
                    v.tensor_copy(sT[:, i * 128:(i + 1) * 128], pSt[i % 2]).then_inc(smSTb, 1)

        if "d_lg" in dbg_o:
            @block.gpsimd
            def _(g):
                g.wait_ge(smRtFm, 8)
                g.dma_start(out=dbg_o["d_lg"], in_=lgfm[:]).then_inc(smDb3, 16)

        rt_ps.close()
        rt_es.close()

        ff_es = ExitStack()
        FF = lambda name, shape, dt: ff_es.enter_context(nc.sbuf_tensor(name, shape, dt))
        W_sb = FF("W_sb", [128, 8 * 8 * D], BF16)
        xh_f = [FF(f"xh_f{i}", [128, 8 * 128], BF16) for i in range(3)]
        x_f = [FF(f"x_f{i}", [128, D], F32) for i in range(2)]
        cv8 = [FF(f"cv8_{i}", [128, 8 * D], BF16) for i in range(2)]
        cv8v = [c[:].rearrange("p (t c) -> p t c", t=8) for c in cv8]
        accA = [FF(f"accA{i}", [128, D], F32) for i in range(2)]
        accB = [FF(f"accB{i}", [128, D], F32) for i in range(2)]

        ff_ps = ExitStack()
        pBias = [ff_ps.enter_context(nc.psum_tensor(f"pBias{i}", [128, 512], F32)) for i in range(2)]
        pE = [ff_ps.enter_context(nc.psum_tensor(f"pE{i}", [128, 512], F32)) for i in range(6)]

        Wv = W_sb[:].rearrange("p (e c f) -> p e c f", e=8, c=8)

        @block.sync
        def _(e):
            e.wait_ge(smW, 16)
            if "d_lg" in dbg_o:
                e.wait_ge(smDb3, 16)
            e.wait_ge(smSTb, NTILE)
            e.wait_ge(smIb, CONV_T_IB)
            e.wait_ge(smIbe, CONV_V_IBE)
            e.wait_ge(smCvSt, CVSTN)
            for ei in range(8):
                for cch in range(8):
                    e.dma_start(out=Wv[:, ei, cch, :],
                                in_=w_bf.ap()[ei, cch * 128:(cch + 1) * 128, :]).then_inc(smFLd, 16)
        WLDN = 8 * 8 * 16

        GROUPS = [((0, 1, 2), (0, 1, 2)), ((3, 4, 5), (3, 4, 5)), ((6, 7), (0, 1))]

        def evac_tok(i, fh, e):
            return i * 16 + fh * 8 + e + 1

        def fld_tok(i):
            return WLDN + (i + 1) * 144

        @block.sync
        def _(e):
            if "d_lg" in dbg_o:
                e.wait_ge(smDb3, 16)
            e.wait_ge(smSTb, NTILE)
            for i in range(NTILE + 2):
                if i < NTILE:
                    if i >= 2:
                        e.wait_ge(smInit, i - 2 + 1)
                    e.dma_start(out=x_f[i % 2][:], in_=x_in[i * 128:(i + 1) * 128, :]).then_inc(smFLd, 16)
                    if i >= 3:
                        e.wait_ge(smGrp, (i - 3 + 1) * 6)
                    for cch in range(8):
                        e.dma_start(
                            out=xh_f[i % 3][:, cch * 128:(cch + 1) * 128],
                            in_=ag1_in.ap()[cch * 128:(cch + 1) * 128,
                                            i * 128:(i + 1) * 128]).then_inc(smFLd, 16)
                pass

        @block.gpsimd
        def _(g):
            h = get_h(g, "gpsimd")
            g.wait_ge(smCC, 3)
            g.wait_ge(smSTb, NTILE)
            ag2w = ag2_out.ap().rearrange("s (hh bt t p) c -> s hh bt p t c",
                                          hh=2, bt=4, p=128)

            def cv8_batch(bt):
                if bt >= 2:
                    g.wait_ge(smInit, (bt - 1) * 8)
                for sl in range(2):
                    g.dma_start(
                        out=cv8v[bt % 2][:, :, sl * 512:(sl + 1) * 512],
                        in_=ag2w[sl, bass.ds(h, 1), bt, :, :, :]).then_inc(smCv8, 16)

            cv8_batch(0)
            cv8_batch(1)
            for i in range(NTILE):
                if i % 8 == 6 and i // 8 + 2 < 4:
                    cv8_batch(i // 8 + 2)
                g.wait_ge(smEvac, evac_tok(i, 1, 7))
                g.dma_start(out=out_t[i * 128:(i + 1) * 128, :], in_=accB[i % 2][:]).then_inc(smOut, 16)

        @block.tensor
        def _(t):
            t.wait_ge(smSTb, NTILE)
            for i in range(NTILE):
                if i >= 1:
                    t.wait_ge(smInit, i)
                t.wait_ge(smFLd, fld_tok(i))
                t.matmul(pBias[0][:], sT[:, i * 128:(i + 1) * 128], bb[:, 0:512],
                         start=True, stop=True)
                t.matmul(pBias[1][:], sT[:, i * 128:(i + 1) * 128], bb[:, 512:1024],
                         start=True, stop=True).then_inc(smBias, 1)
                xhv = xh_f[i % 3][:].rearrange("p (c t) -> p c t", c=8)
                for fh in range(2):
                    for gi, (elist, slots) in enumerate(GROUPS):
                        for e_, s_ in zip(elist, slots):
                            if gi == 2:
                                t.wait_ge(smEvac, evac_tok(i, fh, s_))
                            else:
                                pi, pfh = (i, 0) if fh == 1 else (i - 1, 1)
                                if pi >= 0:
                                    pe = {0: 6, 1: 7, 2: 2, 3: 3, 4: 4, 5: 5}[s_]
                                    t.wait_ge(smEvac, evac_tok(pi, pfh, pe))
                        for dc in range(8):
                            for e_, s_ in zip(elist, slots):
                                ii = t.matmul(pE[s_][:], xhv[:, dc, :],
                                              Wv[:, e_, dc, fh * 512:(fh + 1) * 512],
                                              start=(dc == 0), stop=(dc == 7))
                        ii.then_inc(smGrp, 1)

        @block.vector
        def _(v):
            for i in range(NTILE):
                v.wait_ge(smFLd, fld_tok(i))
                v.wait_ge(smCv8, min(4, i // 8 + 2) * 32)
                v.wait_ge(smBias, i + 1)
                if i >= 2:
                    v.wait_ge(smOut, (i - 2 + 1) * 16)
                for fh in range(2):
                    fcols = slice(fh * 512, (fh + 1) * 512)
                    v.tensor_tensor(accA[i % 2][:, fcols], x_f[i % 2][:, fcols],
                                    cv8v[(i // 8) % 2][:, i % 8, fcols], op=ALU.add)
                    ii = v.tensor_tensor(accB[i % 2][:, fcols], accA[i % 2][:, fcols],
                                         pBias[fh][:], op=ALU.add)
                ii.then_inc(smInit, 1)
                for fh in range(2):
                    fcols = slice(fh * 512, (fh + 1) * 512)
                    src, dst = accB[i % 2], accA[i % 2]
                    for gi, (elist, slots) in enumerate(GROUPS):
                        v.wait_ge(smGrp, i * 6 + fh * 3 + gi + 1)
                        for e_, s_ in zip(elist, slots):
                            if fh == 1 and e_ == 7:
                                v.scalar_tensor_tensor(dst[:, fcols], pE[s_][:],
                                                       s_sl[:, i * 8 + e_:i * 8 + e_ + 1],
                                                       src[:, fcols], op0=ALU.mult,
                                                       op1=ALU.add)
                                v.drain().then_inc(smEvac, 1)
                            else:
                                v.scalar_tensor_tensor(dst[:, fcols], pE[s_][:],
                                                       s_sl[:, i * 8 + e_:i * 8 + e_ + 1],
                                                       src[:, fcols], op0=ALU.mult,
                                                       op1=ALU.add).then_inc(smEvac, 1)
                            src, dst = dst, src

        if dbg_o:
            @block.gpsimd
            def _(g):
                if "d_xh" in dbg_o:
                    g.wait_ge(smLnSt, NTILE * 32)
                    g.dma_start(out=dbg_o["d_xh"], in_=xh_tm.ap()).then_inc(smDbg, 16)
                if "d_xl" in dbg_o:
                    g.wait_ge(smLnSt, NTILE * 32)
                    g.dma_start(out=dbg_o["d_xl"], in_=xl_tm.ap()).then_inc(smDbg, 16)
                if "d_s" in dbg_o:
                    g.wait_ge(smSeT, NTILE)
                    g.dma_start(out=dbg_o["d_s"], in_=s_sl[:]).then_inc(smDbg, 16)
                if "d_lgs" in dbg_o:
                    g.wait_ge(smLgs, NTILE)
                    g.dma_start(out=dbg_o["d_lgs"], in_=lgs[:]).then_inc(smDbg, 16)
                if "d_kf" in dbg_o:
                    g.wait_ge(smKfSt, KFSTN)
                    g.dma_start(out=dbg_o["d_kf"], in_=agk_in.ap()).then_inc(smDbg, 16)
                if "d_conv" in dbg_o:
                    g.wait_ge(smCvSt, CVSTN)
                    g.dma_start(out=dbg_o["d_conv"], in_=conv_hbm.ap()).then_inc(smDbg, 16)
                if "d_ag1" in dbg_o:
                    g.wait_ge(smCC, 1)
                    g.dma_start(out=dbg_o["d_ag1"], in_=ag1_out.ap()[:, 448:512, :]).then_inc(smDbg, 16)
                if "d_agin" in dbg_o:
                    g.wait_ge(smCC, 1)
                    g.dma_start(out=dbg_o["d_agin"], in_=ag1_in.ap()[448:512, :]).then_inc(smDbg, 16)

        @block.sync
        def _(e):
            e.wait_ge(smOut, NTILE * 16)
            if dbg_o:
                e.wait_ge(smDbg, 16 * sum(1 for k in ("d_xh", "d_xl", "d_s", "d_kf", "d_conv", "d_ag1", "d_agin", "d_lgs") if k in dbg_o))
                e.wait_ge(smDb2, n_db2)
                if "d_lg" in dbg_o:
                    e.wait_ge(smDb3, 16)

        ff_ps.close()
        ff_es.close()
        rf_es.close()
        blk.close()
    return nc


_consts_cache = {}


def _get_consts():
    if "c" not in _consts_cache:
        _consts_cache["c"] = _make_consts()
    return _consts_cache["c"]


def make_inmaps(u, ab, h_0, rw, rb, ew, eb):
    cst = _get_consts()
    u = np.ascontiguousarray(np.asarray(u, dtype=np.float32))
    ab = np.asarray(ab, dtype=np.float32)
    h_0 = np.asarray(h_0, dtype=np.float32)
    rw = np.asarray(rw, dtype=np.float32)
    rb = np.asarray(rb, dtype=np.float32)
    ew = np.asarray(ew, dtype=np.float32)
    eb = np.ascontiguousarray(np.asarray(eb, dtype=np.float32))
    rwt = np.ascontiguousarray(rw.T)
    rbb = np.ascontiguousarray(np.broadcast_to(rb[None, :], (128, 8)))
    ewt = np.ascontiguousarray(ew.transpose(0, 2, 1))
    a_co = ab[:D]
    b_co = ab[D:]
    in_maps = []
    for c in range(8):
        b_, h_ = c >> 1, c & 1
        ch0 = h_ * 512 + (c >> 1) * 128
        in_maps.append({
            "x_slab": np.ascontiguousarray(u[b_, h_ * T_CORE:(h_ + 1) * T_CORE, :]),
            "rwt": rwt, "rb_bcast": rbb, "ewt": ewt, "eb": eb,
            "coef_a": np.ascontiguousarray(a_co[ch0:ch0 + 128, :]),
            "coef_b": np.ascontiguousarray(b_co[ch0:ch0 + 128, :]),
            "h0_sl": np.ascontiguousarray(h_0[ch0:ch0 + 128, None]),
            "F1cs_i": cst["F1cs"], "EFW_i": cst["EFW"], "EMK_i": cst["EMK"],
            "FA_i": cst["FA"], "EINV_i": cst["EINV"], "ident_i": cst["ident"],
        })
    return in_maps


def kernel(L0_attn_u, L0_attn_ab, L0_attn_h_0, L0_ffn_router_w, L0_ffn_router_b,
           L0_ffn_expert_w, L0_ffn_expert_b):
    in_maps = make_inmaps(L0_attn_u, L0_attn_ab, L0_attn_h_0, L0_ffn_router_w,
                          L0_ffn_router_b, L0_ffn_expert_w, L0_ffn_expert_b)
    nc = build_program()
    res = run_bass_kernel_spmd(nc, in_maps, core_ids=list(range(8)))
    out = np.empty((B, L, D), np.float32)
    for c in range(8):
        b_, h_ = c >> 1, c & 1
        out[b_, h_ * T_CORE:(h_ + 1) * T_CORE, :] = res.results[c]["out"]
    return out
